# version 07
# baseline (speedup 1.0000x reference)
import numpy as np
from contextlib import ExitStack

import concourse.bass as bass
import concourse.mybir as mybir
from concourse.bass_utils import run_bass_kernel_spmd

F32 = mybir.dt.float32
BF16 = mybir.dt.bfloat16
AF = mybir.ActivationFunctionType
ALU = mybir.AluOpType
AX = mybir.AxisListType

EXP_SCALE = float(np.float32(1.0) / np.sqrt(np.float32(128.0)))
EXP_SHIFT = -10.0
EPS_LN = 1e-5
EPS_RMS = float(np.finfo(np.float32).eps)
G = 64


class Ctr:
    def __init__(self, sem, step=1):
        self.h = sem
        self.n = 0
        self.step = step

    def inc(self, ins):
        ins.then_inc(self.h, self.step)
        self.n += self.step
        return self.n


def build_program(L=4096, D=2048, HID=8192):
    H = D // 128
    NT = L // 128
    NQC = L // 512
    NDT = D // 128
    NHC = HID // 128
    W2F = min(2048, HID)
    NW2C = HID // W2F
    NT1 = HID // 128
    NTT = NT1 + NDT * NW2C
    assert L % 512 == 0 and D % 128 == 0 and HID % 128 == 0 and NT % 4 == 0

    nc = bass.Bass("TRN2", target_bir_lowering=False)
    x_hbm = nc.dram_tensor("x", [L, D], F32, kind="ExternalInput").ap()
    wfc = nc.dram_tensor("wfc", [HID, D], F32, kind="ExternalInput").ap()
    wproj = nc.dram_tensor("wproj", [D, HID], F32, kind="ExternalInput").ap()
    out_hbm = nc.dram_tensor("out", [L, D], F32, kind="ExternalOutput").ap()
    w1t = nc.dram_tensor("w1t", [NHC, 128, D], BF16, kind="Internal").ap()
    w2t = nc.dram_tensor("w2t", [NDT, 128, HID], BF16, kind="Internal").ap()
    ys = nc.dram_tensor("ys", [H, L, 129], F32, kind="Internal").ap()

    ctx = ExitStack()
    with ctx:
        ec = ctx.enter_context

        id_bf = ec(nc.sbuf_tensor("id_bf", [128, 128], BF16))
        id_f = ec(nc.sbuf_tensor("id_f", [128, 128], F32))
        tri = ec(nc.sbuf_tensor("tri", [128, 128], BF16))
        nbias = ec(nc.sbuf_tensor("nbias", [128, 1], F32))
        eps_ln = ec(nc.sbuf_tensor("eps_ln", [128, 1], F32))
        eps_rms = ec(nc.sbuf_tensor("eps_rms", [128, 1], F32))
        ones_bf = ec(nc.sbuf_tensor("ones_bf", [128, 1], BF16))
        ones_f = ec(nc.sbuf_tensor("ones_f", [1, 128], F32))

        sp = Ctr(ec(nc.semaphore("sp")))
        sa = Ctr(ec(nc.semaphore("sa")))
        sv = Ctr(ec(nc.semaphore("sv")))
        sg = Ctr(ec(nc.semaphore("sg")))
        dW = [Ctr(ec(nc.semaphore(f"dW{i}")), 16) for i in range(2)]
        dSCR = [Ctr(ec(nc.semaphore(f"dSCR{i}")), 16) for i in range(2)]
        dX = [Ctr(ec(nc.semaphore(f"dX{i}")), 16) for i in range(2)]
        dYS = [Ctr(ec(nc.semaphore(f"dYS{i}")), 16) for i in range(2)]
        dXY = Ctr(ec(nc.semaphore("dXY")), 16)
        dW1 = [Ctr(ec(nc.semaphore(f"dW1_{i}")), 16) for i in range(2)]
        dW2 = [Ctr(ec(nc.semaphore(f"dW2_{i}")), 16) for i in range(2)]
        dOUT = Ctr(ec(nc.semaphore("dOUT")), 16)

        block = ec(nc.Block())

        chain = {"tensor": sp, "vector": sv, "scalar": sa, "gpsimd": sg}
        marks = {e: 0 for e in chain}

        def snap_marks():
            for e in chain:
                marks[e] = chain[e].n

        def barrier_waits(eng, me):
            for e, c in chain.items():
                if e != me and marks[e] > 0:
                    eng.wait_ge(c.h, marks[e])

        @block.gpsimd
        def _(gp):
            def st(ins):
                sg.inc(ins)
                gp.wait_ge(sg.h, sg.n)
            st(gp.memset(id_bf[:], 0.0))
            st(gp.affine_select(out=id_bf[:], in_=id_bf[:], compare_op=ALU.not_equal,
                                fill=1.0, base=0, pattern=[[-1, 128]], channel_multiplier=1))
            st(gp.memset(id_f[:], 0.0))
            st(gp.affine_select(out=id_f[:], in_=id_f[:], compare_op=ALU.not_equal,
                                fill=1.0, base=0, pattern=[[-1, 128]], channel_multiplier=1))
            st(gp.memset(tri[:], 0.0))
            st(gp.affine_select(out=tri[:], in_=tri[:], compare_op=ALU.is_gt,
                                fill=1.0, base=0, pattern=[[-1, 128]], channel_multiplier=1))
            sg.inc(gp.memset(nbias[:], EXP_SHIFT))
            sg.inc(gp.memset(eps_ln[:], EPS_LN))
            sg.inc(gp.memset(eps_rms[:], EPS_RMS))
            sg.inc(gp.memset(ones_bf[:], 1.0))
            sg.inc(gp.memset(ones_f[:], 1.0))

        INIT_N0 = sg.n

        t_tiles = []
        for i in range(NT1):
            t_tiles.append((wfc[i * 128:(i + 1) * 128, :], w1t[i, :, :], D))
        for dt in range(NDT):
            for c in range(NW2C):
                t_tiles.append((wproj[dt * 128:(dt + 1) * 128, c * W2F:(c + 1) * W2F],
                                w2t[dt, :, c * W2F:(c + 1) * W2F], W2F))
        assert len(t_tiles) == NTT

        s1 = ExitStack()
        with s1:
            e1 = s1.enter_context
            wtf = [e1(nc.sbuf_tensor(f"wtf{i}", [128, 2048], F32)) for i in range(2)]
            wbb = [e1(nc.sbuf_tensor(f"wbb{i}", [128, 2048], BF16)) for i in range(2)]
            tgs = [e1(nc.sbuf_tensor(f"tgs{i}", [128, 2048 // G], F32)) for i in range(2)]
            tsc = [e1(nc.sbuf_tensor(f"tsc{i}", [128, 2048 // G], BF16)) for i in range(2)]
            trc = [e1(nc.sbuf_tensor(f"trc{i}", [128, 2048 // G], F32)) for i in range(2)]
            trr = [e1(nc.sbuf_tensor(f"trr{i}", [128, 2048], BF16)) for i in range(2)]
            tg1 = [e1(nc.sbuf_tensor(f"tg1{i}", [128, 2048], BF16)) for i in range(2)]
            tg2 = [e1(nc.sbuf_tensor(f"tg2{i}", [128, 2048], BF16)) for i in range(2)]
            tqt = [e1(nc.sbuf_tensor(f"tqt{i}", [128, 2048], BF16)) for i in range(4)]
            tst = [e1(nc.sbuf_tensor(f"tst{i}", [128, 16, 128], BF16,
                                       is_transpose_dest=True)) for i in range(2)]
            xres = [e1(nc.sbuf_tensor(f"xres{i}", [128, NT, 129], BF16)) for i in range(2)]
            xT = [e1(nc.sbuf_tensor(f"xT{i}", [128, L], BF16)) for i in range(2)]
            expS = [e1(nc.sbuf_tensor(f"expS{i}", [128, 512], BF16)) for i in range(2)]
            ystg = [e1(nc.sbuf_tensor(f"ystg{i}", [128, 4, 129], F32)) for i in range(2)]
            ps_tr = [e1(nc.psum_tensor(f"ps_tr{i}", [128, 128], BF16)) for i in range(2)]
            ps_s = [e1(nc.psum_tensor(f"ps_s{i}", [128, 512], F32)) for i in range(2)]
            ps_y = [e1(nc.psum_tensor(f"ps_y{i}", [128, 129], F32)) for i in range(4)]

            @block.gpsimd
            def _(gp):
                sg.inc(gp.memset(xres[0][:, :, 128:129], 1.0))
                sg.inc(gp.memset(xres[1][:, :, 128:129], 1.0))

            INIT_N = sg.n

            assert NTT % 2 == 0
            n_pairs = NTT // 2
            head_pairs = [[] for _ in range(H)]
            for p_ in range(n_pairs):
                head_pairs[(p_ * H) // n_pairs].append(p_)

            t_copy_pos = {}
            t_done_pos = {}
            t_ev_end = {}
            t_pe_end = {}
            tr_evict_pos = []
            tr_pe_pos = []

            def t_dve_ops(i, F):
                b = i % 2
                b4 = i % 4
                ng = F // G
                return [
                    lambda v: v.tensor_copy(wbb[b][:, 0:F], wtf[b][:, 0:F]),
                    lambda v: v.tensor_reduce(
                        tgs[b].rearrange("p (g o) -> p g o", o=1)[:, 0:ng, :],
                        wbb[b].rearrange("p (g e) -> p g e", e=G)[:, 0:ng, :],
                        AX.X, ALU.add, apply_absolute_value=True),
                    lambda v: v.tensor_scalar(tsc[b][:, 0:ng], tgs[b][:, 0:ng],
                                              1.0 / G, 1e-8, ALU.mult, ALU.max),
                    lambda v: v.reciprocal(trc[b][:, 0:ng], tsc[b][:, 0:ng]),
                    lambda v: v.tensor_tensor(
                        trr[b].rearrange("p (g e) -> p g e", e=G)[:, 0:ng, :],
                        wbb[b].rearrange("p (g e) -> p g e", e=G)[:, 0:ng, :],
                        trc[b].rearrange("p (g o) -> p g o", o=1)[:, 0:ng, :]
                           .broadcast_to([128, ng, G]),
                        ALU.mult),
                    lambda v: v.tensor_scalar(tg1[b][:, 0:F], trr[b][:, 0:F],
                                              0.5, None, ALU.is_gt),
                    lambda v: v.tensor_scalar(tg2[b][:, 0:F], trr[b][:, 0:F],
                                              -0.5, None, ALU.is_lt),
                    lambda v: v.tensor_sub(tqt[b4][:, 0:F], tg1[b][:, 0:F],
                                           tg2[b][:, 0:F]),
                    lambda v: v.tensor_tensor(
                        tqt[b4].rearrange("p (g e) -> p g e", e=G)[:, 0:ng, :],
                        tqt[b4].rearrange("p (g e) -> p g e", e=G)[:, 0:ng, :],
                        tsc[b].rearrange("p (g o) -> p g o", o=1)[:, 0:ng, :]
                           .broadcast_to([128, ng, G]),
                        ALU.mult),
                ]

            def emit_t_chunk(v, p_):
                ia, ib = 2 * p_, 2 * p_ + 1
                opsa = t_dve_ops(ia, t_tiles[ia][2])
                opsb = t_dve_ops(ib, t_tiles[ib][2])
                v.wait_ge(dW[0].h, 16 * (p_ + 1))
                v.wait_ge(dW[1].h, 16 * (p_ + 1))
                for k in range(9):
                    for i, ops in ((ia, opsa), (ib, opsb)):
                        if k > 0:
                            v.wait_ge(sv.h, sv.n - 1)
                        if k == 7 and i >= 4:
                            v.wait_ge(sp.h, t_pe_end[i - 4])
                        sv.inc(ops[k](v))
                        if k == 0:
                            assert sv.n == t_copy_pos[i]
                        if k == 8:
                            assert sv.n == t_done_pos[i]

            def emit_t_sync_loads(sy, p_):
                for i in (2 * p_, 2 * p_ + 1):
                    if i >= 2:
                        sy.wait_ge(sv.h, t_copy_pos[i - 2])
                    dW[i % 2].inc(sy.dma_start(out=wtf[i % 2][:, 0:t_tiles[i][2]],
                                               in_=t_tiles[i][0]))

            def emit_t_scratch(q, p_):
                for i in (2 * p_, 2 * p_ + 1):
                    F = t_tiles[i][2]
                    q.wait_ge(sv.h, t_ev_end[i])
                    dSCR[i % 2].inc(q.dma_start(out=t_tiles[i][1],
                                                in_=tst[i % 2][:, 0:F // 128, :]))

            def head_x_src(h):
                full = x_hbm[:, h * 128:(h + 1) * 128].rearrange("(a p) d -> p a d", p=128)
                return [full[:, q * (NT // 4):(q + 1) * (NT // 4), :] for q in range(4)]

            @block.gpsimd
            def _(gp):
                barrier_waits(gp, "gpsimd")
                for h in range(min(2, H)):
                    for q, src in enumerate(head_x_src(h)):
                        dX[h % 2].inc(gp.dma_start(
                            out=xres[h % 2][:, q * (NT // 4):(q + 1) * (NT // 4), 0:128],
                            in_=src))

            exp_abs = {}
            mask_abs = {}
            pe_head_end = []
            gks = 0
            n_masks_emitted = 0

            xt_ready = {}
            prev_yev_end = 0

            for h in range(H):
                pe_mark_h = sp.n
                act_mark_h = sa.n
                dve_mark_h = sv.n
                gtr0_h = len(tr_pe_pos)

                hp = head_pairs[h]
                qs = {}
                for idx, p_ in enumerate(hp):
                    if len(hp) <= max(NQC - 2, 1):
                        qc_ = min(2, NQC - len(hp)) + idx
                    else:
                        qc_ = idx // 2
                    qs[p_] = min(qc_, NQC - 1)
                slot_chunks = {qc: [] for qc in range(NQC)}
                for p_ in hp:
                    slot_chunks[qs[p_]].append(p_)
                region_jobs = {qc: [] for qc in range(NQC + 1)}
                for p_ in hp:
                    reg = min(qs[p_] + 1, NQC)
                    for i in (2 * p_, 2 * p_ + 1):
                        for j in range(t_tiles[i][2] // 128):
                            region_jobs[reg].append(("t", i, j))
                pairs_by_region = {qc: [] for qc in range(NQC + 1)}
                for p_ in hp:
                    pairs_by_region[min(qs[p_] + 1, NQC)].append(p_)
                jobs_x = ([("x", h + 1, j) for j in range(NT)] if h + 1 < H else [])
                wts = [5 * (4 * qc + 4) - 6 for qc in range(NQC)]
                tot = sum(wts)
                acc = 0
                taken = 0
                for qc in range(NQC):
                    acc += wts[qc]
                    upto = (len(jobs_x) * acc) // tot
                    region_jobs[qc] = jobs_x[taken:upto] + region_jobs[qc]
                    taken = upto
                prologue_jobs = [("x", 0, j) for j in range(NT)] if h == 0 else []

                pe_seq = {}
                pos = 0
                for j_ in prologue_jobs:
                    pos += 1
                    tr_pe_pos.append(pe_mark_h + pos)
                qc_plan = []
                for qc in range(NQC):
                    nk = 4 * qc + 4
                    pv_lists = [list(range(max(0, kt - 4 * qc), 4)) for kt in range(nk)]
                    jobs = region_jobs[qc]
                    share = [0] * nk
                    for x in range(len(jobs)):
                        share[(x * nk) // len(jobs)] += 1
                    seq = [("s", 0), ("s", 1)]
                    ji = 0
                    for k in range(nk):
                        seq.append(("pv", k))
                        for _ in range(share[k]):
                            seq.append(("tr", jobs[ji])); ji += 1
                        if k + 2 < nk:
                            seq.append(("s", k + 2))
                    assert ji == len(jobs)
                    s_pos = [0] * nk
                    pv_end = [0] * nk
                    for op in seq:
                        if op[0] == "s":
                            pos += 1
                            s_pos[op[1]] = pos
                        elif op[0] == "pv":
                            pos += len(pv_lists[op[1]])
                            pv_end[op[1]] = pos
                        else:
                            pos += 1
                            tr_pe_pos.append(pe_mark_h + pos)
                            kind, a1, a2 = op[1]
                            if kind == "t" and a2 == t_tiles[a1][2] // 128 - 1:
                                t_pe_end[a1] = pe_mark_h + pos
                    pe_seq[qc] = seq
                    qc_plan.append((s_pos, pv_end, pv_lists))
                for op1 in region_jobs[NQC]:
                    pos += 1
                    tr_pe_pos.append(pe_mark_h + pos)
                    kind, a1, a2 = op1
                    if kind == "t" and a2 == t_tiles[a1][2] // 128 - 1:
                        t_pe_end[a1] = pe_mark_h + pos

                a = 0
                ks = gks
                for qc in range(NQC):
                    nk = 4 * qc + 4
                    for k in range(nk):
                        exp_abs[ks + k] = act_mark_h + a + k + 1
                    a += nk
                    ks += nk

                dpos = dve_mark_h
                for j_ in prologue_jobs:
                    dpos += 1
                    tr_evict_pos.append(dpos)
                if h == 0 and prologue_jobs:
                    xt_ready[0] = dpos
                evict_pos = {}
                for qc in range(NQC):
                    for op1 in region_jobs[qc]:
                        dpos += 1
                        tr_evict_pos.append(dpos)
                        kind, a1, a2 = op1
                        if kind == "x" and a2 == NT - 1:
                            xt_ready[a1] = dpos
                        if kind == "t" and a2 == t_tiles[a1][2] // 128 - 1:
                            t_ev_end[a1] = dpos
                    for u in range(4):
                        dpos += 1
                        evict_pos[(qc, u)] = dpos
                    for p_ in slot_chunks[qc]:
                        for k in range(9):
                            for i in (2 * p_, 2 * p_ + 1):
                                dpos += 1
                                if k == 0:
                                    t_copy_pos[i] = dpos
                                if k == 8:
                                    t_done_pos[i] = dpos
                for op1 in region_jobs[NQC]:
                    dpos += 1
                    tr_evict_pos.append(dpos)
                    kind, a1, a2 = op1
                    if kind == "x" and a2 == NT - 1:
                        xt_ready[a1] = dpos
                    if kind == "t" and a2 == t_tiles[a1][2] // 128 - 1:
                        t_ev_end[a1] = dpos

                ks = gks
                nm = n_masks_emitted
                for qc in range(NQC):
                    nk = 4 * qc + 4
                    for t0 in range(4):
                        nm += 1
                        mask_abs[ks + 4 * qc + t0] = INIT_N + nm
                    ks += nk
                n_masks_total_h = nm

                @block.tensor
                def _(t, h=h, qc_plan=qc_plan, pe_mark_h=pe_mark_h, gks=gks,
                      pe_seq=pe_seq, region_jobs=region_jobs,
                      prologue_jobs=prologue_jobs, gtr0_h=gtr0_h,
                      prev_yev_end=prev_yev_end):
                    if h == 0:
                        barrier_waits(t, "tensor")
                        t.wait_ge(dX[0].h, 16 * 4)
                    g = gtr0_h
                    seen_x = set()
                    seen_t = set()

                    def emit_tr(op1, g):
                        kind, a1, a2 = op1
                        if kind == "x":
                            if a1 not in seen_x:
                                t.wait_ge(dX[a1 % 2].h, 16 * 4 * (a1 // 2 + 1))
                                seen_x.add(a1)
                            src = xres[a1 % 2][:, a2, 0:128]
                        else:
                            if a1 not in seen_t:
                                t.wait_ge(sv.h, t_done_pos[a1])
                                seen_t.add(a1)
                            src = tqt[a1 % 4][:, a2 * 128:(a2 + 1) * 128]
                        if g >= 2:
                            t.wait_ge(sv.h, tr_evict_pos[g - 2])
                        sp.inc(t.transpose(ps_tr[g % 2][:], src, id_bf[:]))
                        assert sp.n == tr_pe_pos[g]

                    for op1 in prologue_jobs:
                        emit_tr(op1, g); g += 1
                    ks = gks
                    for qc in range(NQC):
                        nk = 4 * qc + 4
                        s_pos, pv_end, pv_lists = qc_plan[qc]

                        def emit_s(k, ks=ks, qc=qc):
                            kk = ks + k
                            if kk >= 2:
                                t.wait_ge(sa.h, exp_abs[kk - 2])
                            if k == 0 and qc == 0:
                                t.wait_ge(sv.h, xt_ready[h])
                            qlo = 128 * max(0, k - 4 * qc)
                            sp.inc(t.matmul(ps_s[kk % 2][:, qlo:512],
                                            xT[h % 2][:, k * 128:(k + 1) * 128],
                                            xT[h % 2][:, qc * 512 + qlo: qc * 512 + 512],
                                            start=True, stop=True))

                        def emit_pv(k, ks=ks, qc=qc, pv_lists=pv_lists):
                            kk = ks + k
                            t0 = k - 4 * qc
                            first = True
                            for u in pv_lists[k]:
                                if first:
                                    t.wait_ge(sa.h, exp_abs[kk])
                                    if k == 0 and qc >= 1:
                                        t.wait_ge(sv.h, evict_pos[(qc - 1, 3)])
                                    if k == 0 and qc == 0 and prev_yev_end > 0:
                                        t.wait_ge(sv.h, prev_yev_end)
                                    if t0 >= 0:
                                        t.wait_ge(sg.h, mask_abs[kk])
                                    first = False
                                sp.inc(t.matmul(ps_y[u][:],
                                                expS[kk % 2][:, u * 128:(u + 1) * 128],
                                                xres[h % 2][:, k, 0:129],
                                                start=(k == 0), stop=(k == 4 * qc + u)))

                        for op in pe_seq[qc]:
                            if op[0] == "s":
                                emit_s(op[1])
                            elif op[0] == "pv":
                                emit_pv(op[1])
                            else:
                                emit_tr(op[1], g); g += 1
                        ks += nk
                    for op1 in region_jobs[NQC]:
                        emit_tr(op1, g); g += 1

                @block.scalar
                def _(s, h=h, qc_plan=qc_plan, pe_mark_h=pe_mark_h, gks=gks):
                    if h == 0:
                        barrier_waits(s, "scalar")
                    ks = gks
                    for qc in range(NQC):
                        nk = 4 * qc + 4
                        s_pos, pv_end, pv_lists = qc_plan[qc]
                        for k in range(nk):
                            kk = ks + k
                            qlo = 128 * max(0, k - 4 * qc)
                            s.wait_ge(sp.h, pe_mark_h + s_pos[k])
                            if kk - 2 in mask_abs:
                                s.wait_ge(sg.h, mask_abs[kk - 2])
                            sa.inc(s.activation(expS[kk % 2][:, qlo:512],
                                                ps_s[kk % 2][:, qlo:512],
                                                AF.Exp, bias=nbias[:], scale=EXP_SCALE))
                        ks += nk

                @block.vector
                def _(v, h=h, qc_plan=qc_plan, pe_mark_h=pe_mark_h,
                      region_jobs=region_jobs, prologue_jobs=prologue_jobs,
                      gtr0_h=gtr0_h, evict_pos=evict_pos,
                      slot_chunks=slot_chunks):
                    if h == 0:
                        barrier_waits(v, "vector")
                        v.wait_ge(sg.h, INIT_N)
                    g = gtr0_h
                    seen_t = set()

                    def emit_cp(op1, g):
                        kind, a1, a2 = op1
                        v.wait_ge(sp.h, tr_pe_pos[g])
                        if kind == "x":
                            sv.inc(v.tensor_copy(xT[a1 % 2][:, a2 * 128:(a2 + 1) * 128],
                                                 ps_tr[g % 2][:]))
                        else:
                            if a1 not in seen_t and a1 >= 2:
                                v.wait_ge(dSCR[a1 % 2].h, 16 * ((a1 - 2) // 2 + 1))
                            seen_t.add(a1)
                            sv.inc(v.tensor_copy(tst[a1 % 2][:, a2, :],
                                                 ps_tr[g % 2][:]))
                        assert sv.n == tr_evict_pos[g]

                    for op1 in prologue_jobs:
                        emit_cp(op1, g); g += 1
                    for qc in range(NQC):
                        s_pos, pv_end, pv_lists = qc_plan[qc]
                        for op1 in region_jobs[qc]:
                            emit_cp(op1, g); g += 1
                        for u in range(4):
                            v.wait_ge(sp.h, pe_mark_h + pv_end[4 * qc + u])
                            idx_par = (NQC // 2) * h + qc // 2
                            if idx_par >= 1:
                                v.wait_ge(dYS[qc % 2].h, 16 * idx_par)
                            sv.inc(v.tensor_copy(ystg[qc % 2][:, u, :], ps_y[u][:]))
                            assert sv.n == evict_pos[(qc, u)]
                        for p_ in slot_chunks[qc]:
                            emit_t_chunk(v, p_)
                    for op1 in region_jobs[NQC]:
                        emit_cp(op1, g); g += 1

                @block.gpsimd
                def _(gp, h=h, gks=gks, evict_pos=evict_pos,
                      pairs_by_region=pairs_by_region):
                    ks = gks
                    for qc in range(NQC):
                        nk = 4 * qc + 4
                        for t0 in range(4):
                            kk = ks + 4 * qc + t0
                            gp.wait_ge(sa.h, exp_abs[kk])
                            ins = gp.tensor_tensor(
                                expS[kk % 2][:, t0 * 128:(t0 + 1) * 128],
                                expS[kk % 2][:, t0 * 128:(t0 + 1) * 128],
                                tri[:], ALU.mult)
                            sg.inc(ins)
                            assert sg.n == mask_abs[kk]
                        gp.wait_ge(sv.h, evict_pos[(qc, 3)])
                        dYS[qc % 2].inc(gp.dma_start(
                            out=ys[h].rearrange("(a p) e -> p a e", p=128)
                            [:, 4 * qc:4 * qc + 4, :],
                            in_=ystg[qc % 2][:, :, :]))
                        ks += nk

                @block.sync
                def _(sy, h=h):
                    if h == 0:
                        barrier_waits(sy, "sync")
                    items = []
                    for p_ in head_pairs[h]:
                        gl = max((t_copy_pos[i - 2] for i in (2 * p_, 2 * p_ + 1)
                                  if i >= 2), default=0)
                        items.append((gl, 0, p_))
                        gs_ = max(t_ev_end[i] for i in (2 * p_, 2 * p_ + 1))
                        items.append((gs_, 1, p_))
                    for _, kind2, p_ in sorted(items):
                        if kind2 == 0:
                            emit_t_sync_loads(sy, p_)
                        else:
                            emit_t_scratch(sy, p_)

                gks += sum(4 * qc + 4 for qc in range(NQC))
                n_masks_emitted = n_masks_total_h
                pe_head_end.append(sp.n)
                prev_yev_end = evict_pos[(NQC - 1, 3)]

                if h + 2 < H:
                    @block.gpsimd
                    def _(gp, h=h):
                        gp.wait_ge(sp.h, pe_head_end[h])
                        for q, src in enumerate(head_x_src(h + 2)):
                            dX[h % 2].inc(gp.dma_start(
                                out=xres[h % 2][:, q * (NT // 4):(q + 1) * (NT // 4), 0:128],
                                in_=src))

            snap_marks()

        s2 = ExitStack()
        with s2:
            e2 = s2.enter_context
            xb = e2(nc.sbuf_tensor("xb", [128, 4, D], F32))
            yb = e2(nc.sbuf_tensor("yb", [128, 4, H, 129], F32))
            rcp16 = e2(nc.sbuf_tensor("rcp16", [128, H], F32))
            X1T = e2(nc.sbuf_tensor("X1T", [128, NDT, 512], BF16))
            hbuf = e2(nc.sbuf_tensor("hbuf", [128, NHC, 512], BF16))
            w1b = [e2(nc.sbuf_tensor(f"w1b{i}", [128, D], BF16)) for i in range(2)]
            w2b = [e2(nc.sbuf_tensor(f"w2b{i}", [128, HID // 2], BF16)) for i in range(2)]
            trel = [e2(nc.sbuf_tensor(f"trel{i}", [128, 512], BF16)) for i in range(2)]
            hsq = [e2(nc.sbuf_tensor(f"hsq{i}", [128, 512], BF16)) for i in range(2)]
            lnt = e2(nc.sbuf_tensor("lnt", [1, 512], F32))
            rstd_row = e2(nc.sbuf_tensor("rstd_row", [1, 512], F32))
            rstd_b = e2(nc.sbuf_tensor("rstd_b", [128, 512], F32))
            yTd = [e2(nc.sbuf_tensor(f"yTd{i}", [128, 512], F32)) for i in range(2)]
            sqb = e2(nc.sbuf_tensor("sqb", [128, D], F32))
            s1c = e2(nc.sbuf_tensor("s1c", [128, 1], F32))
            s2c = e2(nc.sbuf_tensor("s2c", [128, 1], F32))
            mc = e2(nc.sbuf_tensor("mc", [128, 1], F32))
            m2c = e2(nc.sbuf_tensor("m2c", [128, 1], F32))
            vc = e2(nc.sbuf_tensor("vc", [128, 1], F32))
            rsc = e2(nc.sbuf_tensor("rsc", [128, 1], F32))
            ps_h = [e2(nc.psum_tensor(f"ps_h{i}", [128, 512], F32)) for i in range(2)]
            ps_m = [e2(nc.psum_tensor(f"ps_m{i}", [128, 512], F32)) for i in range(2)]
            ps_den = e2(nc.psum_tensor("ps_den", [128, 512], F32))
            ps_bc = e2(nc.psum_tensor("ps_bc", [128, 512], F32))
            ps_tf = [e2(nc.psum_tensor(f"ps_tf{i}", [128, 128], F32)) for i in range(2)]

            for c in range(NQC):
                pe_b, sa_b, sv_b, dxy_b = sp.n, sa.n, sv.n, dXY.n

                @block.sync
                def _(sy, c=c):
                    barrier_waits(sy, "sync")
                    if c == 0:
                        sy.wait_ge(dYS[0].h, 16 * (NQC // 2) * H)
                        sy.wait_ge(dYS[1].h, 16 * (NQC // 2) * H)
                    if c >= 1:
                        sy.wait_ge(dOUT.h, 16 * c)
                    dXY.inc(sy.dma_start(
                        out=xb[:, :, :],
                        in_=x_hbm[c * 512:(c + 1) * 512, :]
                        .rearrange("(a p) d -> p a d", p=128)))
                    for tp in range(4):
                        for h in range(H):
                            dXY.inc(sy.dma_start(
                                out=yb[:, tp, h, :],
                                in_=ys[h].rearrange("(a p) e -> p a e", p=128)
                                [:, 4 * c + tp, :]))

                def bp(tp, j, sv_b=sv_b, H=H):
                    return sv_b + tp * (H + 9) + j

                @block.vector
                def _(v, c=c, dxy_b=dxy_b):
                    def st(ins):
                        sv.inc(ins)
                        v.wait_ge(sv.h, sv.n)
                    barrier_waits(v, "vector")
                    v.wait_ge(dXY.h, dxy_b + 16 * (1 + 4 * H))
                    for tp in range(4):
                        st(v.reciprocal(rcp16[:, :], yb[:, tp, :, 128]))
                        for h in range(H):
                            st(v.scalar_tensor_tensor(
                                xb[:, tp, h * 128:(h + 1) * 128],
                                yb[:, tp, h, 0:128],
                                rcp16[:, h:h + 1],
                                xb[:, tp, h * 128:(h + 1) * 128],
                                ALU.mult, ALU.add))
                        st(v.tensor_reduce(s1c[:], xb[:, tp, :], AX.X, ALU.add))
                        st(v.tensor_scalar(mc[:], s1c[:], 1.0 / D, None, ALU.mult))
                        st(v.tensor_tensor(sqb[:, :], xb[:, tp, :], xb[:, tp, :],
                                           ALU.mult))
                        st(v.tensor_reduce(s2c[:], sqb[:, :], AX.X, ALU.add))
                        st(v.tensor_scalar(s2c[:], s2c[:], 1.0 / D, None, ALU.mult))
                        st(v.tensor_tensor(m2c[:], mc[:], mc[:], ALU.mult))
                        st(v.tensor_sub(vc[:], s2c[:], m2c[:]))
                        v.wait_ge(sa.h, sa_b + 2 * tp + 2)
                        st(v.tensor_scalar(xb[:, tp, :], xb[:, tp, :], mc[:], rsc[:],
                                           ALU.subtract, ALU.mult))

                @block.scalar
                def _(s, c=c):
                    def st(ins):
                        sa.inc(ins)
                        s.wait_ge(sa.h, sa.n)
                    barrier_waits(s, "scalar")
                    for tp in range(4):
                        s.wait_ge(sv.h, bp(tp, H + 8))
                        st(s.activation(vc[:], vc[:], AF.Ln, bias=eps_ln[:]))
                        st(s.activation(rsc[:], vc[:], AF.Exp, scale=-0.5))

                pe_bT = sp.n

                @block.tensor
                def _(t, c=c):
                    barrier_waits(t, "tensor")
                    g = 0
                    for tq in range(4):
                        for dt in range(NDT):
                            if dt == 0:
                                t.wait_ge(sv.h, bp(tq, H + 9))
                            if g >= 2:
                                t.wait_ge(sa.h, sa_b + 8 + g - 1)
                            sp.inc(t.transpose(ps_tf[g % 2][:],
                                               xb[:, tq, dt * 128:(dt + 1) * 128],
                                               id_f[:]))
                            g += 1

                @block.scalar
                def _(s, c=c, pe_bT=pe_bT):
                    g = 0
                    for tq in range(4):
                        for dt in range(NDT):
                            s.wait_ge(sp.h, pe_bT + g + 1)
                            sa.inc(s.copy(X1T[:, dt, tq * 128:(tq + 1) * 128],
                                          ps_tf[g % 2][:]))
                            g += 1

                snap_marks()

                pe_c, sa_c, sv_c = sp.n, sa.n, sv.n
                dw1_c = [dW1[0].n, dW1[1].n]
                dw2_c = [dW2[0].n, dW2[1].n]

                pe_ops = []
                for hc in range(NHC):
                    pe_ops.append(("m1", hc))
                    if hc >= 1:
                        pe_ops.append(("den", hc - 1))
                pe_ops.append(("den", NHC - 1))
                pe_ops.append(("bcast", 0))
                tr_order = []
                for dt in range(NDT):
                    pe_ops.append(("m2", dt))
                    if dt >= 2:
                        pe_ops.append(("tr", dt - 2))
                        tr_order.append(dt - 2)
                pe_ops.append(("tr", NDT - 2)); tr_order.append(NDT - 2)
                pe_ops.append(("tr", NDT - 1)); tr_order.append(NDT - 1)

                m1_end = {}
                den_pos = {}
                m2_end = {}
                trp = {}
                bcast_pos = 0
                p = 0
                for kind, idx in pe_ops:
                    if kind == "m1":
                        p += NDT; m1_end[idx] = p
                    elif kind == "den":
                        p += 1; den_pos[idx] = p
                    elif kind == "bcast":
                        p += 1; bcast_pos = p
                    elif kind == "m2":
                        p += NHC; m2_end[idx] = p
                    elif kind == "tr":
                        for tq in range(4):
                            p += 1; trp[(idx, tq)] = p

                def d_tt(hc, sv_c=sv_c):
                    return sv_c + 2 * hc + 1

                def d_hsq(hc, sv_c=sv_c):
                    return sv_c + 2 * hc + 2

                d_bc = sv_c + 2 * NHC + 1

                def d_ev(dt, d_bc=d_bc):
                    return d_bc + 5 * dt + 1

                def d_add(dt, tq, d_bc=d_bc):
                    return d_bc + 5 * dt + 2 + tq

                d_addend = d_bc + 5 * NDT

                def d_ln2(tq, j, d_addend=d_addend):
                    return d_addend + 8 * tq + j

                def a_relu(hc, sa_c=sa_c):
                    return sa_c + hc + 1

                a_rms_done = sa_c + NHC + 2

                def a_ln2(tq, a_rms_done=a_rms_done):
                    return a_rms_done + 2 * tq + 1

                @block.sync
                def _(sy, c=c):
                    barrier_waits(sy, "sync")
                    if c == 0:
                        sy.wait_ge(dSCR[0].h, 16 * ((NTT + 1) // 2))
                        sy.wait_ge(dSCR[1].h, 16 * (NTT // 2))
                    for hc in range(NHC):
                        if hc >= 2:
                            sy.wait_ge(sp.h, pe_c + m1_end[hc - 2])
                        dW1[hc % 2].inc(sy.dma_start(out=w1b[hc % 2][:, :], in_=w1t[hc, :, :]))
                    for dt in range(NDT):
                        for half in range(2):
                            if dt >= 1:
                                prev = m2_end[dt - 1] - (NHC // 2 if half == 0 else 0)
                                sy.wait_ge(sp.h, pe_c + prev)
                            dW2[half].inc(sy.dma_start(
                                out=w2b[half][:, :],
                                in_=w2t[dt, :, half * (HID // 2):(half + 1) * (HID // 2)]))

                @block.tensor
                def _(t, c=c):
                    barrier_waits(t, "tensor")
                    for kind, idx in pe_ops:
                        if kind == "m1":
                            hc = idx
                            t.wait_ge(dW1[hc % 2].h, dw1_c[hc % 2] + 16 * (hc // 2 + 1))
                            if hc >= 2:
                                t.wait_ge(sa.h, a_relu(hc - 2))
                            for dt in range(NDT):
                                sp.inc(t.matmul(ps_h[hc % 2][:, :],
                                                w1b[hc % 2][:, dt * 128:(dt + 1) * 128],
                                                X1T[:, dt, :],
                                                start=(dt == 0), stop=(dt == NDT - 1)))
                        elif kind == "den":
                            hc = idx
                            t.wait_ge(sv.h, d_hsq(hc))
                            sp.inc(t.matmul(ps_den[0:1, :], ones_bf[:],
                                            hsq[hc % 2][:, :],
                                            start=(hc == 0), stop=(hc == NHC - 1)))
                        elif kind == "bcast":
                            t.wait_ge(sa.h, a_rms_done)
                            sp.inc(t.matmul(ps_bc[:, :], ones_f[:, :], rstd_row[:, :],
                                            start=True, stop=True))
                        elif kind == "m2":
                            dt = idx
                            for hc in range(NHC):
                                half = 0 if hc < NHC // 2 else 1
                                if hc == 0 or hc == NHC // 2:
                                    t.wait_ge(dW2[half].h, dw2_c[half] + 16 * (dt + 1))
                                if hc == 0 and dt >= 2:
                                    t.wait_ge(sv.h, d_ev(dt - 2))
                                off = hc - half * (NHC // 2)
                                sp.inc(t.matmul(ps_m[dt % 2][:, :],
                                                w2b[half][:, off * 128:(off + 1) * 128],
                                                hbuf[:, hc, :],
                                                start=(hc == 0), stop=(hc == NHC - 1)))
                        elif kind == "tr":
                            dt = idx
                            oi = tr_order.index(dt)
                            for tq in range(4):
                                t.wait_ge(sv.h, d_ev(dt))
                                gt = 4 * oi + tq
                                if gt >= 2:
                                    pdt, ptq = tr_order[(gt - 2) // 4], (gt - 2) % 4
                                    t.wait_ge(sv.h, d_add(pdt, ptq))
                                sp.inc(t.transpose(ps_tf[tq % 2][:],
                                                   yTd[dt % 2][:, tq * 128:(tq + 1) * 128],
                                                   id_f[:]))

                @block.scalar
                def _(s, c=c):
                    def st(ins):
                        sa.inc(ins)
                        s.wait_ge(sa.h, sa.n)
                    barrier_waits(s, "scalar")
                    for hc in range(NHC):
                        s.wait_ge(sp.h, pe_c + m1_end[hc])
                        if hc >= 2:
                            s.wait_ge(sv.h, d_tt(hc - 2))
                        sa.inc(s.activation(trel[hc % 2][:, :], ps_h[hc % 2][:, :],
                                            AF.Relu))
                    s.wait_ge(sp.h, pe_c + den_pos[NHC - 1])
                    st(s.activation(lnt[:, :], ps_den[0:1, :], AF.Ln,
                                    bias=eps_rms[0:1, :], scale=1.0 / HID))
                    st(s.activation(rstd_row[:, :], lnt[:, :], AF.Exp, scale=-0.5))
                    for tq in range(4):
                        s.wait_ge(sv.h, d_ln2(tq, 7))
                        st(s.activation(vc[:], vc[:], AF.Ln, bias=eps_ln[:]))
                        st(s.activation(rsc[:], vc[:], AF.Exp, scale=-0.5))

                @block.vector
                def _(v, c=c):
                    def st(ins):
                        sv.inc(ins)
                        v.wait_ge(sv.h, sv.n)
                    barrier_waits(v, "vector")
                    for hc in range(NHC):
                        v.wait_ge(sa.h, a_relu(hc))
                        if hc >= 2:
                            v.wait_ge(sp.h, pe_c + den_pos[hc - 2])
                        st(v.tensor_tensor(hbuf[:, hc, :], trel[hc % 2][:, :],
                                           trel[hc % 2][:, :], ALU.mult))
                        st(v.tensor_tensor(hsq[hc % 2][:, :], hbuf[:, hc, :],
                                           hbuf[:, hc, :], ALU.mult))
                    v.wait_ge(sp.h, pe_c + bcast_pos)
                    st(v.tensor_copy(rstd_b[:, :], ps_bc[:, :]))
                    for dt in range(NDT):
                        v.wait_ge(sp.h, pe_c + m2_end[dt])
                        if dt >= 2:
                            v.wait_ge(sp.h, pe_c + trp[(dt - 2, 3)])
                        st(v.tensor_tensor(yTd[dt % 2][:, :], ps_m[dt % 2][:, :],
                                           rstd_b[:, :], ALU.mult))
                        for tq in range(4):
                            v.wait_ge(sp.h, pe_c + trp[(dt, tq)])
                            st(v.tensor_tensor(xb[:, tq, dt * 128:(dt + 1) * 128],
                                               ps_tf[tq % 2][:],
                                               xb[:, tq, dt * 128:(dt + 1) * 128],
                                               ALU.add))
                    for tq in range(4):
                        st(v.tensor_reduce(s1c[:], xb[:, tq, :], AX.X, ALU.add))
                        st(v.tensor_scalar(mc[:], s1c[:], 1.0 / D, None, ALU.mult))
                        st(v.tensor_tensor(sqb[:, :], xb[:, tq, :], xb[:, tq, :],
                                           ALU.mult))
                        st(v.tensor_reduce(s2c[:], sqb[:, :], AX.X, ALU.add))
                        st(v.tensor_scalar(s2c[:], s2c[:], 1.0 / D, None, ALU.mult))
                        st(v.tensor_tensor(m2c[:], mc[:], mc[:], ALU.mult))
                        st(v.tensor_sub(vc[:], s2c[:], m2c[:]))
                        v.wait_ge(sa.h, a_rms_done + 2 * (tq + 1))
                        st(v.tensor_scalar(xb[:, tq, :], xb[:, tq, :], mc[:], rsc[:],
                                           ALU.subtract, ALU.mult))

                @block.gpsimd
                def _(gp, c=c):
                    gp.wait_ge(sv.h, d_ln2(3, 8))
                    dOUT.inc(gp.dma_start(
                        out=out_hbm[c * 512:(c + 1) * 512, :]
                        .rearrange("(a p) d -> p a d", p=128),
                        in_=xb[:, :, :]))

                snap_marks()

        @block.gpsimd
        def _(gp):
            gp.wait_ge(dOUT.h, 16 * NQC)

    return nc


def kernel(L0_attn_x, L0_ffn_W_fc, L0_ffn_W_proj):
    x = np.ascontiguousarray(np.asarray(L0_attn_x, dtype=np.float32))
    wfc = np.ascontiguousarray(np.asarray(L0_ffn_W_fc, dtype=np.float32))
    wproj = np.ascontiguousarray(np.asarray(L0_ffn_W_proj, dtype=np.float32))
    B = x.shape[0]
    nc = build_program(L=x.shape[1], D=x.shape[2], HID=wfc.shape[0])
    in_maps = [{"x": x[b], "wfc": wfc, "wproj": wproj} for b in range(B)]
    res = run_bass_kernel_spmd(nc, in_maps, core_ids=list(range(B)))
    return np.stack([res.results[b]["out"] for b in range(B)], axis=0)


# version 08
# speedup vs baseline: 1.0151x; 1.0151x over previous
import numpy as np
from contextlib import ExitStack

import concourse.bass as bass
import concourse.mybir as mybir
from concourse.bass_utils import run_bass_kernel_spmd

F32 = mybir.dt.float32
BF16 = mybir.dt.bfloat16
AF = mybir.ActivationFunctionType
ALU = mybir.AluOpType
AX = mybir.AxisListType

EXP_SCALE = float(np.float32(1.0) / np.sqrt(np.float32(128.0)))
EXP_SHIFT = -10.0
EPS_LN = 1e-5
EPS_RMS = float(np.finfo(np.float32).eps)
G = 64


class Ctr:
    def __init__(self, sem, step=1):
        self.h = sem
        self.n = 0
        self.step = step

    def inc(self, ins):
        ins.then_inc(self.h, self.step)
        self.n += self.step
        return self.n


def build_program(L=4096, D=2048, HID=8192):
    H = D // 128
    NT = L // 128
    NQC = L // 512
    NDT = D // 128
    NHC = HID // 128
    W2F = min(2048, HID)
    NW2C = HID // W2F
    NT1 = HID // 128
    NTT = NT1 + NDT * NW2C
    assert L % 512 == 0 and D % 128 == 0 and HID % 128 == 0 and NT % 4 == 0

    nc = bass.Bass("TRN2", target_bir_lowering=False)
    x_hbm = nc.dram_tensor("x", [L, D], F32, kind="ExternalInput").ap()
    wfc = nc.dram_tensor("wfc", [HID, D], F32, kind="ExternalInput").ap()
    wproj = nc.dram_tensor("wproj", [D, HID], F32, kind="ExternalInput").ap()
    out_hbm = nc.dram_tensor("out", [L, D], F32, kind="ExternalOutput").ap()
    w1t = nc.dram_tensor("w1t", [NHC, 128, D], BF16, kind="Internal").ap()
    w2t = nc.dram_tensor("w2t", [NDT, 128, HID], BF16, kind="Internal").ap()
    ys = nc.dram_tensor("ys", [H, L, 129], F32, kind="Internal").ap()

    ctx = ExitStack()
    with ctx:
        ec = ctx.enter_context

        id_bf = ec(nc.sbuf_tensor("id_bf", [128, 128], BF16))
        id_f = ec(nc.sbuf_tensor("id_f", [128, 128], F32))
        tri = ec(nc.sbuf_tensor("tri", [128, 128], BF16))
        nbias = ec(nc.sbuf_tensor("nbias", [128, 1], F32))
        eps_ln = ec(nc.sbuf_tensor("eps_ln", [128, 1], F32))
        eps_rms = ec(nc.sbuf_tensor("eps_rms", [128, 1], F32))
        ones_bf = ec(nc.sbuf_tensor("ones_bf", [128, 1], BF16))
        ones_f = ec(nc.sbuf_tensor("ones_f", [1, 128], F32))

        sp = Ctr(ec(nc.semaphore("sp")))
        sa = Ctr(ec(nc.semaphore("sa")))
        sv = Ctr(ec(nc.semaphore("sv")))
        sg = Ctr(ec(nc.semaphore("sg")))
        dW = [Ctr(ec(nc.semaphore(f"dW{i}")), 16) for i in range(2)]
        dSCR = [Ctr(ec(nc.semaphore(f"dSCR{i}")), 16) for i in range(2)]
        dX = [Ctr(ec(nc.semaphore(f"dX{i}")), 16) for i in range(2)]
        dYS = [Ctr(ec(nc.semaphore(f"dYS{i}")), 16) for i in range(2)]
        dXY = Ctr(ec(nc.semaphore("dXY")), 16)
        dW1 = [Ctr(ec(nc.semaphore(f"dW1_{i}")), 16) for i in range(4)]
        dW2 = [Ctr(ec(nc.semaphore(f"dW2_{i}")), 16) for i in range(2)]
        dOUT = Ctr(ec(nc.semaphore("dOUT")), 16)

        block = ec(nc.Block())

        chain = {"tensor": sp, "vector": sv, "scalar": sa, "gpsimd": sg}
        marks = {e: 0 for e in chain}

        def snap_marks():
            for e in chain:
                marks[e] = chain[e].n

        def barrier_waits(eng, me):
            for e, c in chain.items():
                if e != me and marks[e] > 0:
                    eng.wait_ge(c.h, marks[e])

        @block.gpsimd
        def _(gp):
            def st(ins):
                sg.inc(ins)
                gp.wait_ge(sg.h, sg.n)
            st(gp.memset(id_bf[:], 0.0))
            st(gp.affine_select(out=id_bf[:], in_=id_bf[:], compare_op=ALU.not_equal,
                                fill=1.0, base=0, pattern=[[-1, 128]], channel_multiplier=1))
            st(gp.memset(id_f[:], 0.0))
            st(gp.affine_select(out=id_f[:], in_=id_f[:], compare_op=ALU.not_equal,
                                fill=1.0, base=0, pattern=[[-1, 128]], channel_multiplier=1))
            st(gp.memset(tri[:], 0.0))
            st(gp.affine_select(out=tri[:], in_=tri[:], compare_op=ALU.is_gt,
                                fill=1.0, base=0, pattern=[[-1, 128]], channel_multiplier=1))
            sg.inc(gp.memset(nbias[:], EXP_SHIFT))
            sg.inc(gp.memset(eps_ln[:], EPS_LN))
            sg.inc(gp.memset(eps_rms[:], EPS_RMS))
            sg.inc(gp.memset(ones_bf[:], 1.0))
            sg.inc(gp.memset(ones_f[:], 1.0))

        INIT_N0 = sg.n

        t_tiles = []
        for i in range(NT1):
            t_tiles.append((wfc[i * 128:(i + 1) * 128, :], w1t[i, :, :], D))
        for dt in range(NDT):
            for c in range(NW2C):
                t_tiles.append((wproj[dt * 128:(dt + 1) * 128, c * W2F:(c + 1) * W2F],
                                w2t[dt, :, c * W2F:(c + 1) * W2F], W2F))
        assert len(t_tiles) == NTT

        s1 = ExitStack()
        with s1:
            e1 = s1.enter_context
            wtf = [e1(nc.sbuf_tensor(f"wtf{i}", [128, 2048], F32)) for i in range(2)]
            wbb = [e1(nc.sbuf_tensor(f"wbb{i}", [128, 2048], BF16)) for i in range(2)]
            tgs = [e1(nc.sbuf_tensor(f"tgs{i}", [128, 2048 // G], F32)) for i in range(2)]
            tsc = [e1(nc.sbuf_tensor(f"tsc{i}", [128, 2048 // G], BF16)) for i in range(2)]
            trc = [e1(nc.sbuf_tensor(f"trc{i}", [128, 2048 // G], F32)) for i in range(2)]
            trr = [e1(nc.sbuf_tensor(f"trr{i}", [128, 2048], BF16)) for i in range(2)]
            tg1 = [e1(nc.sbuf_tensor(f"tg1{i}", [128, 2048], BF16)) for i in range(2)]
            tg2 = [e1(nc.sbuf_tensor(f"tg2{i}", [128, 2048], BF16)) for i in range(2)]
            tqt = [e1(nc.sbuf_tensor(f"tqt{i}", [128, 2048], BF16)) for i in range(4)]
            tst = [e1(nc.sbuf_tensor(f"tst{i}", [128, 16, 128], BF16,
                                       is_transpose_dest=True)) for i in range(2)]
            xres = [e1(nc.sbuf_tensor(f"xres{i}", [128, NT, 129], BF16)) for i in range(2)]
            xT = [e1(nc.sbuf_tensor(f"xT{i}", [128, L], BF16)) for i in range(2)]
            expS = [e1(nc.sbuf_tensor(f"expS{i}", [128, 512], BF16)) for i in range(2)]
            ystg = [e1(nc.sbuf_tensor(f"ystg{i}", [128, 4, 129], F32)) for i in range(2)]
            ps_tr = [e1(nc.psum_tensor(f"ps_tr{i}", [128, 128], BF16)) for i in range(2)]
            ps_s = [e1(nc.psum_tensor(f"ps_s{i}", [128, 512], F32)) for i in range(2)]
            ps_y = [e1(nc.psum_tensor(f"ps_y{i}", [128, 129], F32)) for i in range(4)]

            @block.gpsimd
            def _(gp):
                sg.inc(gp.memset(xres[0][:, :, 128:129], 1.0))
                sg.inc(gp.memset(xres[1][:, :, 128:129], 1.0))

            INIT_N = sg.n

            assert NTT % 2 == 0
            n_pairs = NTT // 2
            head_pairs = [[] for _ in range(H)]
            for p_ in range(n_pairs):
                head_pairs[(p_ * H) // n_pairs].append(p_)

            t_copy_pos = {}
            t_done_pos = {}
            t_ev_end = {}
            t_pe_end = {}
            tr_evict_pos = []
            tr_pe_pos = []

            def t_dve_ops(i, F):
                b = i % 2
                b4 = i % 4
                ng = F // G
                return [
                    lambda v: v.tensor_copy(wbb[b][:, 0:F], wtf[b][:, 0:F]),
                    lambda v: v.tensor_reduce(
                        tgs[b].rearrange("p (g o) -> p g o", o=1)[:, 0:ng, :],
                        wbb[b].rearrange("p (g e) -> p g e", e=G)[:, 0:ng, :],
                        AX.X, ALU.add, apply_absolute_value=True),
                    lambda v: v.tensor_scalar(tsc[b][:, 0:ng], tgs[b][:, 0:ng],
                                              1.0 / G, 1e-8, ALU.mult, ALU.max),
                    lambda v: v.reciprocal(trc[b][:, 0:ng], tsc[b][:, 0:ng]),
                    lambda v: v.tensor_tensor(
                        trr[b].rearrange("p (g e) -> p g e", e=G)[:, 0:ng, :],
                        wbb[b].rearrange("p (g e) -> p g e", e=G)[:, 0:ng, :],
                        trc[b].rearrange("p (g o) -> p g o", o=1)[:, 0:ng, :]
                           .broadcast_to([128, ng, G]),
                        ALU.mult),
                    lambda v: v.tensor_scalar(tg1[b][:, 0:F], trr[b][:, 0:F],
                                              0.5, None, ALU.is_gt),
                    lambda v: v.tensor_scalar(tg2[b][:, 0:F], trr[b][:, 0:F],
                                              -0.5, None, ALU.is_lt),
                    lambda v: v.tensor_sub(tqt[b4][:, 0:F], tg1[b][:, 0:F],
                                           tg2[b][:, 0:F]),
                    lambda v: v.tensor_tensor(
                        tqt[b4].rearrange("p (g e) -> p g e", e=G)[:, 0:ng, :],
                        tqt[b4].rearrange("p (g e) -> p g e", e=G)[:, 0:ng, :],
                        tsc[b].rearrange("p (g o) -> p g o", o=1)[:, 0:ng, :]
                           .broadcast_to([128, ng, G]),
                        ALU.mult),
                ]

            def emit_t_chunk(v, p_):
                ia, ib = 2 * p_, 2 * p_ + 1
                opsa = t_dve_ops(ia, t_tiles[ia][2])
                opsb = t_dve_ops(ib, t_tiles[ib][2])
                v.wait_ge(dW[0].h, 16 * (p_ + 1))
                v.wait_ge(dW[1].h, 16 * (p_ + 1))
                for k in range(9):
                    for i, ops in ((ia, opsa), (ib, opsb)):
                        if k > 0:
                            v.wait_ge(sv.h, sv.n - 1)
                        if k == 7 and i >= 4:
                            v.wait_ge(sp.h, t_pe_end[i - 4])
                        sv.inc(ops[k](v))
                        if k == 0:
                            assert sv.n == t_copy_pos[i]
                        if k == 8:
                            assert sv.n == t_done_pos[i]

            def emit_t_sync_loads(sy, p_):
                for i in (2 * p_, 2 * p_ + 1):
                    if i >= 2:
                        sy.wait_ge(sv.h, t_copy_pos[i - 2])
                    dW[i % 2].inc(sy.dma_start(out=wtf[i % 2][:, 0:t_tiles[i][2]],
                                               in_=t_tiles[i][0]))

            def emit_t_scratch(q, p_):
                for i in (2 * p_, 2 * p_ + 1):
                    F = t_tiles[i][2]
                    q.wait_ge(sv.h, t_ev_end[i])
                    dSCR[i % 2].inc(q.dma_start(out=t_tiles[i][1],
                                                in_=tst[i % 2][:, 0:F // 128, :]))

            def head_x_src(h):
                full = x_hbm[:, h * 128:(h + 1) * 128].rearrange("(a p) d -> p a d", p=128)
                return [full[:, q * (NT // 4):(q + 1) * (NT // 4), :] for q in range(4)]

            @block.gpsimd
            def _(gp):
                barrier_waits(gp, "gpsimd")
                for h in range(min(2, H)):
                    for q, src in enumerate(head_x_src(h)):
                        dX[h % 2].inc(gp.dma_start(
                            out=xres[h % 2][:, q * (NT // 4):(q + 1) * (NT // 4), 0:128],
                            in_=src))

            exp_abs = {}
            mask_abs = {}
            pe_head_end = []
            gks = 0
            n_masks_emitted = 0

            xt_ready = {}
            prev_yev_end = 0

            for h in range(H):
                pe_mark_h = sp.n
                act_mark_h = sa.n
                dve_mark_h = sv.n
                gtr0_h = len(tr_pe_pos)

                hp = head_pairs[h]
                qs = {}
                for idx, p_ in enumerate(hp):
                    if len(hp) <= max(NQC - 2, 1):
                        qc_ = min(2, NQC - len(hp)) + idx
                    else:
                        qc_ = idx // 2
                    qs[p_] = min(qc_, NQC - 1)
                slot_chunks = {qc: [] for qc in range(NQC)}
                for p_ in hp:
                    slot_chunks[qs[p_]].append(p_)
                region_jobs = {qc: [] for qc in range(NQC + 1)}
                for p_ in hp:
                    reg = min(qs[p_] + 1, NQC)
                    for i in (2 * p_, 2 * p_ + 1):
                        for j in range(t_tiles[i][2] // 128):
                            region_jobs[reg].append(("t", i, j))
                pairs_by_region = {qc: [] for qc in range(NQC + 1)}
                for p_ in hp:
                    pairs_by_region[min(qs[p_] + 1, NQC)].append(p_)
                jobs_x = ([("x", h + 1, j) for j in range(NT)] if h + 1 < H else [])
                wts = [5 * (4 * qc + 4) - 6 for qc in range(NQC)]
                tot = sum(wts)
                acc = 0
                taken = 0
                for qc in range(NQC):
                    acc += wts[qc]
                    upto = (len(jobs_x) * acc) // tot
                    region_jobs[qc] = jobs_x[taken:upto] + region_jobs[qc]
                    taken = upto
                prologue_jobs = [("x", 0, j) for j in range(NT)] if h == 0 else []

                pe_seq = {}
                pos = 0
                for j_ in prologue_jobs:
                    pos += 1
                    tr_pe_pos.append(pe_mark_h + pos)
                qc_plan = []
                for qc in range(NQC):
                    nk = 4 * qc + 4
                    pv_lists = [list(range(max(0, kt - 4 * qc), 4)) for kt in range(nk)]
                    jobs = region_jobs[qc]
                    share = [0] * nk
                    for x in range(len(jobs)):
                        share[(x * nk) // len(jobs)] += 1
                    seq = [("s", 0), ("s", 1)]
                    ji = 0
                    for k in range(nk):
                        seq.append(("pv", k))
                        for _ in range(share[k]):
                            seq.append(("tr", jobs[ji])); ji += 1
                        if k + 2 < nk:
                            seq.append(("s", k + 2))
                    assert ji == len(jobs)
                    s_pos = [0] * nk
                    pv_end = [0] * nk
                    for op in seq:
                        if op[0] == "s":
                            pos += 1
                            s_pos[op[1]] = pos
                        elif op[0] == "pv":
                            pos += len(pv_lists[op[1]])
                            pv_end[op[1]] = pos
                        else:
                            pos += 1
                            tr_pe_pos.append(pe_mark_h + pos)
                            kind, a1, a2 = op[1]
                            if kind == "t" and a2 == t_tiles[a1][2] // 128 - 1:
                                t_pe_end[a1] = pe_mark_h + pos
                    pe_seq[qc] = seq
                    qc_plan.append((s_pos, pv_end, pv_lists))
                for op1 in region_jobs[NQC]:
                    pos += 1
                    tr_pe_pos.append(pe_mark_h + pos)
                    kind, a1, a2 = op1
                    if kind == "t" and a2 == t_tiles[a1][2] // 128 - 1:
                        t_pe_end[a1] = pe_mark_h + pos

                a = 0
                ks = gks
                for qc in range(NQC):
                    nk = 4 * qc + 4
                    for k in range(nk):
                        exp_abs[ks + k] = act_mark_h + a + k + 1
                    a += nk
                    ks += nk

                dpos = dve_mark_h
                for j_ in prologue_jobs:
                    dpos += 1
                    tr_evict_pos.append(dpos)
                if h == 0 and prologue_jobs:
                    xt_ready[0] = dpos
                evict_pos = {}
                for qc in range(NQC):
                    for op1 in region_jobs[qc]:
                        dpos += 1
                        tr_evict_pos.append(dpos)
                        kind, a1, a2 = op1
                        if kind == "x" and a2 == NT - 1:
                            xt_ready[a1] = dpos
                        if kind == "t" and a2 == t_tiles[a1][2] // 128 - 1:
                            t_ev_end[a1] = dpos
                    for u in range(4):
                        dpos += 1
                        evict_pos[(qc, u)] = dpos
                    for p_ in slot_chunks[qc]:
                        for k in range(9):
                            for i in (2 * p_, 2 * p_ + 1):
                                dpos += 1
                                if k == 0:
                                    t_copy_pos[i] = dpos
                                if k == 8:
                                    t_done_pos[i] = dpos
                for op1 in region_jobs[NQC]:
                    dpos += 1
                    tr_evict_pos.append(dpos)
                    kind, a1, a2 = op1
                    if kind == "x" and a2 == NT - 1:
                        xt_ready[a1] = dpos
                    if kind == "t" and a2 == t_tiles[a1][2] // 128 - 1:
                        t_ev_end[a1] = dpos

                ks = gks
                nm = n_masks_emitted
                for qc in range(NQC):
                    nk = 4 * qc + 4
                    for t0 in range(4):
                        nm += 1
                        mask_abs[ks + 4 * qc + t0] = INIT_N + nm
                    ks += nk
                n_masks_total_h = nm

                @block.tensor
                def _(t, h=h, qc_plan=qc_plan, pe_mark_h=pe_mark_h, gks=gks,
                      pe_seq=pe_seq, region_jobs=region_jobs,
                      prologue_jobs=prologue_jobs, gtr0_h=gtr0_h,
                      prev_yev_end=prev_yev_end):
                    if h == 0:
                        barrier_waits(t, "tensor")
                        t.wait_ge(dX[0].h, 16 * 4)
                    g = gtr0_h
                    seen_x = set()
                    seen_t = set()

                    def emit_tr(op1, g):
                        kind, a1, a2 = op1
                        if kind == "x":
                            if a1 not in seen_x:
                                t.wait_ge(dX[a1 % 2].h, 16 * 4 * (a1 // 2 + 1))
                                seen_x.add(a1)
                            src = xres[a1 % 2][:, a2, 0:128]
                        else:
                            if a1 not in seen_t:
                                t.wait_ge(sv.h, t_done_pos[a1])
                                seen_t.add(a1)
                            src = tqt[a1 % 4][:, a2 * 128:(a2 + 1) * 128]
                        if g >= 2:
                            t.wait_ge(sv.h, tr_evict_pos[g - 2])
                        sp.inc(t.transpose(ps_tr[g % 2][:], src, id_bf[:]))
                        assert sp.n == tr_pe_pos[g]

                    for op1 in prologue_jobs:
                        emit_tr(op1, g); g += 1
                    ks = gks
                    for qc in range(NQC):
                        nk = 4 * qc + 4
                        s_pos, pv_end, pv_lists = qc_plan[qc]

                        def emit_s(k, ks=ks, qc=qc):
                            kk = ks + k
                            if kk >= 2:
                                t.wait_ge(sa.h, exp_abs[kk - 2])
                            if k == 0 and qc == 0:
                                t.wait_ge(sv.h, xt_ready[h])
                            qlo = 128 * max(0, k - 4 * qc)
                            sp.inc(t.matmul(ps_s[kk % 2][:, qlo:512],
                                            xT[h % 2][:, k * 128:(k + 1) * 128],
                                            xT[h % 2][:, qc * 512 + qlo: qc * 512 + 512],
                                            start=True, stop=True))

                        def emit_pv(k, ks=ks, qc=qc, pv_lists=pv_lists):
                            kk = ks + k
                            t0 = k - 4 * qc
                            first = True
                            for u in pv_lists[k]:
                                if first:
                                    t.wait_ge(sa.h, exp_abs[kk])
                                    if k == 0 and qc >= 1:
                                        t.wait_ge(sv.h, evict_pos[(qc - 1, 3)])
                                    if k == 0 and qc == 0 and prev_yev_end > 0:
                                        t.wait_ge(sv.h, prev_yev_end)
                                    if t0 >= 0:
                                        t.wait_ge(sg.h, mask_abs[kk])
                                    first = False
                                sp.inc(t.matmul(ps_y[u][:],
                                                expS[kk % 2][:, u * 128:(u + 1) * 128],
                                                xres[h % 2][:, k, 0:129],
                                                start=(k == 0), stop=(k == 4 * qc + u)))

                        for op in pe_seq[qc]:
                            if op[0] == "s":
                                emit_s(op[1])
                            elif op[0] == "pv":
                                emit_pv(op[1])
                            else:
                                emit_tr(op[1], g); g += 1
                        ks += nk
                    for op1 in region_jobs[NQC]:
                        emit_tr(op1, g); g += 1

                @block.scalar
                def _(s, h=h, qc_plan=qc_plan, pe_mark_h=pe_mark_h, gks=gks):
                    if h == 0:
                        barrier_waits(s, "scalar")
                    ks = gks
                    for qc in range(NQC):
                        nk = 4 * qc + 4
                        s_pos, pv_end, pv_lists = qc_plan[qc]
                        for k in range(nk):
                            kk = ks + k
                            qlo = 128 * max(0, k - 4 * qc)
                            s.wait_ge(sp.h, pe_mark_h + s_pos[k])
                            if kk - 2 in mask_abs:
                                s.wait_ge(sg.h, mask_abs[kk - 2])
                            sa.inc(s.activation(expS[kk % 2][:, qlo:512],
                                                ps_s[kk % 2][:, qlo:512],
                                                AF.Exp, bias=nbias[:], scale=EXP_SCALE))
                        ks += nk

                @block.vector
                def _(v, h=h, qc_plan=qc_plan, pe_mark_h=pe_mark_h,
                      region_jobs=region_jobs, prologue_jobs=prologue_jobs,
                      gtr0_h=gtr0_h, evict_pos=evict_pos,
                      slot_chunks=slot_chunks):
                    if h == 0:
                        barrier_waits(v, "vector")
                        v.wait_ge(sg.h, INIT_N)
                    g = gtr0_h
                    seen_t = set()

                    def emit_cp(op1, g):
                        kind, a1, a2 = op1
                        v.wait_ge(sp.h, tr_pe_pos[g])
                        if kind == "x":
                            sv.inc(v.tensor_copy(xT[a1 % 2][:, a2 * 128:(a2 + 1) * 128],
                                                 ps_tr[g % 2][:]))
                        else:
                            if a1 not in seen_t and a1 >= 2:
                                v.wait_ge(dSCR[a1 % 2].h, 16 * ((a1 - 2) // 2 + 1))
                            seen_t.add(a1)
                            sv.inc(v.tensor_copy(tst[a1 % 2][:, a2, :],
                                                 ps_tr[g % 2][:]))
                        assert sv.n == tr_evict_pos[g]

                    for op1 in prologue_jobs:
                        emit_cp(op1, g); g += 1
                    for qc in range(NQC):
                        s_pos, pv_end, pv_lists = qc_plan[qc]
                        for op1 in region_jobs[qc]:
                            emit_cp(op1, g); g += 1
                        for u in range(4):
                            v.wait_ge(sp.h, pe_mark_h + pv_end[4 * qc + u])
                            idx_par = (NQC // 2) * h + qc // 2
                            if idx_par >= 1:
                                v.wait_ge(dYS[qc % 2].h, 16 * idx_par)
                            sv.inc(v.tensor_copy(ystg[qc % 2][:, u, :], ps_y[u][:]))
                            assert sv.n == evict_pos[(qc, u)]
                        for p_ in slot_chunks[qc]:
                            emit_t_chunk(v, p_)
                    for op1 in region_jobs[NQC]:
                        emit_cp(op1, g); g += 1

                @block.gpsimd
                def _(gp, h=h, gks=gks, evict_pos=evict_pos,
                      pairs_by_region=pairs_by_region):
                    ks = gks
                    for qc in range(NQC):
                        nk = 4 * qc + 4
                        for t0 in range(4):
                            kk = ks + 4 * qc + t0
                            gp.wait_ge(sa.h, exp_abs[kk])
                            ins = gp.tensor_tensor(
                                expS[kk % 2][:, t0 * 128:(t0 + 1) * 128],
                                expS[kk % 2][:, t0 * 128:(t0 + 1) * 128],
                                tri[:], ALU.mult)
                            sg.inc(ins)
                            assert sg.n == mask_abs[kk]
                        gp.wait_ge(sv.h, evict_pos[(qc, 3)])
                        dYS[qc % 2].inc(gp.dma_start(
                            out=ys[h].rearrange("(a p) e -> p a e", p=128)
                            [:, 4 * qc:4 * qc + 4, :],
                            in_=ystg[qc % 2][:, :, :]))
                        ks += nk

                @block.sync
                def _(sy, h=h):
                    if h == 0:
                        barrier_waits(sy, "sync")
                    items = []
                    for p_ in head_pairs[h]:
                        gl = max((t_copy_pos[i - 2] for i in (2 * p_, 2 * p_ + 1)
                                  if i >= 2), default=0)
                        items.append((gl, 0, p_))
                        gs_ = max(t_ev_end[i] for i in (2 * p_, 2 * p_ + 1))
                        items.append((gs_, 1, p_))
                    for _, kind2, p_ in sorted(items):
                        if kind2 == 0:
                            emit_t_sync_loads(sy, p_)
                        else:
                            emit_t_scratch(sy, p_)

                gks += sum(4 * qc + 4 for qc in range(NQC))
                n_masks_emitted = n_masks_total_h
                pe_head_end.append(sp.n)
                prev_yev_end = evict_pos[(NQC - 1, 3)]

                if h + 2 < H:
                    @block.gpsimd
                    def _(gp, h=h):
                        gp.wait_ge(sp.h, pe_head_end[h])
                        for q, src in enumerate(head_x_src(h + 2)):
                            dX[h % 2].inc(gp.dma_start(
                                out=xres[h % 2][:, q * (NT // 4):(q + 1) * (NT // 4), 0:128],
                                in_=src))

            snap_marks()

        s2 = ExitStack()
        with s2:
            e2 = s2.enter_context
            xb = e2(nc.sbuf_tensor("xb", [128, 4, D], F32))
            yb = e2(nc.sbuf_tensor("yb", [128, 4, H, 129], F32))
            rcp16 = e2(nc.sbuf_tensor("rcp16", [128, H], F32))
            X1T = e2(nc.sbuf_tensor("X1T", [128, NDT, 512], BF16))
            hbuf = e2(nc.sbuf_tensor("hbuf", [128, NHC, 512], BF16))
            w1b = [e2(nc.sbuf_tensor(f"w1b{i}", [128, D], BF16)) for i in range(4)]
            w2b = [e2(nc.sbuf_tensor(f"w2b{i}", [128, HID // 2], BF16)) for i in range(2)]
            trel = [e2(nc.sbuf_tensor(f"trel{i}", [128, 512], BF16)) for i in range(2)]
            hsq = [e2(nc.sbuf_tensor(f"hsq{i}", [128, 512], BF16)) for i in range(2)]
            lnt = e2(nc.sbuf_tensor("lnt", [1, 512], F32))
            rstd_row = e2(nc.sbuf_tensor("rstd_row", [1, 512], F32))
            rstd_b = e2(nc.sbuf_tensor("rstd_b", [128, 512], F32))
            yTd = [e2(nc.sbuf_tensor(f"yTd{i}", [128, 512], F32)) for i in range(2)]
            sqb = e2(nc.sbuf_tensor("sqb", [128, D], F32))
            s1c = e2(nc.sbuf_tensor("s1c", [128, 1], F32))
            s2c = e2(nc.sbuf_tensor("s2c", [128, 1], F32))
            mc = e2(nc.sbuf_tensor("mc", [128, 1], F32))
            m2c = e2(nc.sbuf_tensor("m2c", [128, 1], F32))
            vc = e2(nc.sbuf_tensor("vc", [128, 1], F32))
            rsc = e2(nc.sbuf_tensor("rsc", [128, 1], F32))
            ps_h = [e2(nc.psum_tensor(f"ps_h{i}", [128, 512], F32)) for i in range(2)]
            ps_m = [e2(nc.psum_tensor(f"ps_m{i}", [128, 512], F32)) for i in range(2)]
            ps_den = e2(nc.psum_tensor("ps_den", [128, 512], F32))
            ps_bc = e2(nc.psum_tensor("ps_bc", [128, 512], F32))
            ps_tf = [e2(nc.psum_tensor(f"ps_tf{i}", [128, 128], F32)) for i in range(2)]

            for c in range(NQC):
                pe_b, sa_b, sv_b, dxy_b = sp.n, sa.n, sv.n, dXY.n

                @block.sync
                def _(sy, c=c):
                    barrier_waits(sy, "sync")
                    if c == 0:
                        sy.wait_ge(dYS[0].h, 16 * (NQC // 2) * H)
                        sy.wait_ge(dYS[1].h, 16 * (NQC // 2) * H)
                    if c >= 1:
                        sy.wait_ge(dOUT.h, 16 * c)
                    dXY.inc(sy.dma_start(
                        out=xb[:, :, :],
                        in_=x_hbm[c * 512:(c + 1) * 512, :]
                        .rearrange("(a p) d -> p a d", p=128)))
                    for tp in range(4):
                        for h in range(H):
                            dXY.inc(sy.dma_start(
                                out=yb[:, tp, h, :],
                                in_=ys[h].rearrange("(a p) e -> p a e", p=128)
                                [:, 4 * c + tp, :]))

                def bp(tp, j, sv_b=sv_b, H=H):
                    return sv_b + tp * (H + 9) + j

                @block.vector
                def _(v, c=c, dxy_b=dxy_b):
                    def st(ins):
                        sv.inc(ins)
                        v.wait_ge(sv.h, sv.n)
                    barrier_waits(v, "vector")
                    v.wait_ge(dXY.h, dxy_b + 16 * (1 + 4 * H))
                    for tp in range(4):
                        st(v.reciprocal(rcp16[:, :], yb[:, tp, :, 128]))
                        for h in range(H):
                            st(v.scalar_tensor_tensor(
                                xb[:, tp, h * 128:(h + 1) * 128],
                                yb[:, tp, h, 0:128],
                                rcp16[:, h:h + 1],
                                xb[:, tp, h * 128:(h + 1) * 128],
                                ALU.mult, ALU.add))
                        st(v.tensor_reduce(s1c[:], xb[:, tp, :], AX.X, ALU.add))
                        st(v.tensor_scalar(mc[:], s1c[:], 1.0 / D, None, ALU.mult))
                        st(v.tensor_tensor(sqb[:, :], xb[:, tp, :], xb[:, tp, :],
                                           ALU.mult))
                        st(v.tensor_reduce(s2c[:], sqb[:, :], AX.X, ALU.add))
                        st(v.tensor_scalar(s2c[:], s2c[:], 1.0 / D, None, ALU.mult))
                        st(v.tensor_tensor(m2c[:], mc[:], mc[:], ALU.mult))
                        st(v.tensor_sub(vc[:], s2c[:], m2c[:]))
                        v.wait_ge(sa.h, sa_b + 2 * tp + 2)
                        st(v.tensor_scalar(xb[:, tp, :], xb[:, tp, :], mc[:], rsc[:],
                                           ALU.subtract, ALU.mult))

                @block.scalar
                def _(s, c=c):
                    def st(ins):
                        sa.inc(ins)
                        s.wait_ge(sa.h, sa.n)
                    barrier_waits(s, "scalar")
                    for tp in range(4):
                        s.wait_ge(sv.h, bp(tp, H + 8))
                        st(s.activation(vc[:], vc[:], AF.Ln, bias=eps_ln[:]))
                        st(s.activation(rsc[:], vc[:], AF.Exp, scale=-0.5))

                pe_bT = sp.n

                @block.tensor
                def _(t, c=c):
                    barrier_waits(t, "tensor")
                    g = 0
                    for tq in range(4):
                        for dt in range(NDT):
                            if dt == 0:
                                t.wait_ge(sv.h, bp(tq, H + 9))
                            if g >= 2:
                                t.wait_ge(sa.h, sa_b + 8 + g - 1)
                            sp.inc(t.transpose(ps_tf[g % 2][:],
                                               xb[:, tq, dt * 128:(dt + 1) * 128],
                                               id_f[:]))
                            g += 1

                @block.scalar
                def _(s, c=c, pe_bT=pe_bT):
                    g = 0
                    for tq in range(4):
                        for dt in range(NDT):
                            s.wait_ge(sp.h, pe_bT + g + 1)
                            sa.inc(s.copy(X1T[:, dt, tq * 128:(tq + 1) * 128],
                                          ps_tf[g % 2][:]))
                            g += 1

                snap_marks()

                pe_c, sa_c, sv_c = sp.n, sa.n, sv.n
                dw1_c = [c_.n for c_ in dW1]
                dw2_c = [dW2[0].n, dW2[1].n]

                pe_ops = []
                for hc in range(NHC):
                    pe_ops.append(("m1", hc))
                    if hc >= 1:
                        pe_ops.append(("den", hc - 1))
                pe_ops.append(("den", NHC - 1))
                pe_ops.append(("bcast", 0))
                tr_order = []
                for dt in range(NDT):
                    pe_ops.append(("m2", dt))
                    if dt >= 2:
                        pe_ops.append(("tr", dt - 2))
                        tr_order.append(dt - 2)
                pe_ops.append(("tr", NDT - 2)); tr_order.append(NDT - 2)
                pe_ops.append(("tr", NDT - 1)); tr_order.append(NDT - 1)

                m1_end = {}
                den_pos = {}
                m2_end = {}
                trp = {}
                bcast_pos = 0
                p = 0
                for kind, idx in pe_ops:
                    if kind == "m1":
                        p += NDT; m1_end[idx] = p
                    elif kind == "den":
                        p += 1; den_pos[idx] = p
                    elif kind == "bcast":
                        p += 1; bcast_pos = p
                    elif kind == "m2":
                        p += NHC; m2_end[idx] = p
                    elif kind == "tr":
                        for tq in range(4):
                            p += 1; trp[(idx, tq)] = p

                def d_tt(hc, sv_c=sv_c):
                    return sv_c + 2 * hc + 1

                def d_hsq(hc, sv_c=sv_c):
                    return sv_c + 2 * hc + 2

                d_bc = sv_c + 2 * NHC + 1

                def d_ev(dt, d_bc=d_bc):
                    return d_bc + 5 * dt + 1

                def d_add(dt, tq, d_bc=d_bc):
                    return d_bc + 5 * dt + 2 + tq

                d_addend = d_bc + 5 * NDT

                def d_ln2(tq, j, d_addend=d_addend):
                    return d_addend + 8 * tq + j

                def a_relu(hc, sa_c=sa_c):
                    return sa_c + hc + 1

                a_rms_done = sa_c + NHC + 2

                def a_ln2(tq, a_rms_done=a_rms_done):
                    return a_rms_done + 2 * tq + 1

                @block.sync
                def _(sy, c=c):
                    barrier_waits(sy, "sync")
                    if c == 0:
                        sy.wait_ge(dSCR[0].h, 16 * ((NTT + 1) // 2))
                        sy.wait_ge(dSCR[1].h, 16 * (NTT // 2))
                    for hc in range(NHC):
                        if hc >= 4:
                            sy.wait_ge(sp.h, pe_c + m1_end[hc - 4])
                        dW1[hc % 4].inc(sy.dma_start(out=w1b[hc % 4][:, :], in_=w1t[hc, :, :]))
                    for dt in range(NDT):
                        for half in range(2):
                            if dt >= 1:
                                prev = m2_end[dt - 1] - (NHC // 2 if half == 0 else 0)
                                sy.wait_ge(sp.h, pe_c + prev)
                            dW2[half].inc(sy.dma_start(
                                out=w2b[half][:, :],
                                in_=w2t[dt, :, half * (HID // 2):(half + 1) * (HID // 2)]))

                @block.tensor
                def _(t, c=c):
                    barrier_waits(t, "tensor")
                    for kind, idx in pe_ops:
                        if kind == "m1":
                            hc = idx
                            t.wait_ge(dW1[hc % 4].h, dw1_c[hc % 4] + 16 * (hc // 4 + 1))
                            if hc >= 2:
                                t.wait_ge(sa.h, a_relu(hc - 2))
                            for dt in range(NDT):
                                sp.inc(t.matmul(ps_h[hc % 2][:, :],
                                                w1b[hc % 4][:, dt * 128:(dt + 1) * 128],
                                                X1T[:, dt, :],
                                                start=(dt == 0), stop=(dt == NDT - 1)))
                        elif kind == "den":
                            hc = idx
                            t.wait_ge(sv.h, d_hsq(hc))
                            sp.inc(t.matmul(ps_den[0:1, :], ones_bf[:],
                                            hsq[hc % 2][:, :],
                                            start=(hc == 0), stop=(hc == NHC - 1)))
                        elif kind == "bcast":
                            t.wait_ge(sa.h, a_rms_done)
                            sp.inc(t.matmul(ps_bc[:, :], ones_f[:, :], rstd_row[:, :],
                                            start=True, stop=True))
                        elif kind == "m2":
                            dt = idx
                            for hc in range(NHC):
                                half = 0 if hc < NHC // 2 else 1
                                if hc == 0 or hc == NHC // 2:
                                    t.wait_ge(dW2[half].h, dw2_c[half] + 16 * (dt + 1))
                                if hc == 0 and dt >= 2:
                                    t.wait_ge(sv.h, d_ev(dt - 2))
                                off = hc - half * (NHC // 2)
                                sp.inc(t.matmul(ps_m[dt % 2][:, :],
                                                w2b[half][:, off * 128:(off + 1) * 128],
                                                hbuf[:, hc, :],
                                                start=(hc == 0), stop=(hc == NHC - 1)))
                        elif kind == "tr":
                            dt = idx
                            oi = tr_order.index(dt)
                            for tq in range(4):
                                t.wait_ge(sv.h, d_ev(dt))
                                gt = 4 * oi + tq
                                if gt >= 2:
                                    pdt, ptq = tr_order[(gt - 2) // 4], (gt - 2) % 4
                                    t.wait_ge(sv.h, d_add(pdt, ptq))
                                sp.inc(t.transpose(ps_tf[tq % 2][:],
                                                   yTd[dt % 2][:, tq * 128:(tq + 1) * 128],
                                                   id_f[:]))

                @block.scalar
                def _(s, c=c):
                    def st(ins):
                        sa.inc(ins)
                        s.wait_ge(sa.h, sa.n)
                    barrier_waits(s, "scalar")
                    for hc in range(NHC):
                        s.wait_ge(sp.h, pe_c + m1_end[hc])
                        if hc >= 2:
                            s.wait_ge(sv.h, d_tt(hc - 2))
                        sa.inc(s.activation(trel[hc % 2][:, :], ps_h[hc % 2][:, :],
                                            AF.Relu))
                    s.wait_ge(sp.h, pe_c + den_pos[NHC - 1])
                    st(s.activation(lnt[:, :], ps_den[0:1, :], AF.Ln,
                                    bias=eps_rms[0:1, :], scale=1.0 / HID))
                    st(s.activation(rstd_row[:, :], lnt[:, :], AF.Exp, scale=-0.5))
                    for tq in range(4):
                        s.wait_ge(sv.h, d_ln2(tq, 7))
                        st(s.activation(vc[:], vc[:], AF.Ln, bias=eps_ln[:]))
                        st(s.activation(rsc[:], vc[:], AF.Exp, scale=-0.5))

                @block.vector
                def _(v, c=c):
                    def st(ins):
                        sv.inc(ins)
                        v.wait_ge(sv.h, sv.n)
                    barrier_waits(v, "vector")
                    for hc in range(NHC):
                        v.wait_ge(sa.h, a_relu(hc))
                        if hc >= 2:
                            v.wait_ge(sp.h, pe_c + den_pos[hc - 2])
                        st(v.tensor_tensor(hbuf[:, hc, :], trel[hc % 2][:, :],
                                           trel[hc % 2][:, :], ALU.mult))
                        st(v.tensor_tensor(hsq[hc % 2][:, :], hbuf[:, hc, :],
                                           hbuf[:, hc, :], ALU.mult))
                    v.wait_ge(sp.h, pe_c + bcast_pos)
                    st(v.tensor_copy(rstd_b[:, :], ps_bc[:, :]))
                    for dt in range(NDT):
                        v.wait_ge(sp.h, pe_c + m2_end[dt])
                        if dt >= 2:
                            v.wait_ge(sp.h, pe_c + trp[(dt - 2, 3)])
                        st(v.tensor_tensor(yTd[dt % 2][:, :], ps_m[dt % 2][:, :],
                                           rstd_b[:, :], ALU.mult))
                        for tq in range(4):
                            v.wait_ge(sp.h, pe_c + trp[(dt, tq)])
                            st(v.tensor_tensor(xb[:, tq, dt * 128:(dt + 1) * 128],
                                               ps_tf[tq % 2][:],
                                               xb[:, tq, dt * 128:(dt + 1) * 128],
                                               ALU.add))
                    for tq in range(4):
                        st(v.tensor_reduce(s1c[:], xb[:, tq, :], AX.X, ALU.add))
                        st(v.tensor_scalar(mc[:], s1c[:], 1.0 / D, None, ALU.mult))
                        st(v.tensor_tensor(sqb[:, :], xb[:, tq, :], xb[:, tq, :],
                                           ALU.mult))
                        st(v.tensor_reduce(s2c[:], sqb[:, :], AX.X, ALU.add))
                        st(v.tensor_scalar(s2c[:], s2c[:], 1.0 / D, None, ALU.mult))
                        st(v.tensor_tensor(m2c[:], mc[:], mc[:], ALU.mult))
                        st(v.tensor_sub(vc[:], s2c[:], m2c[:]))
                        v.wait_ge(sa.h, a_rms_done + 2 * (tq + 1))
                        st(v.tensor_scalar(xb[:, tq, :], xb[:, tq, :], mc[:], rsc[:],
                                           ALU.subtract, ALU.mult))

                @block.gpsimd
                def _(gp, c=c):
                    gp.wait_ge(sv.h, d_ln2(3, 8))
                    dOUT.inc(gp.dma_start(
                        out=out_hbm[c * 512:(c + 1) * 512, :]
                        .rearrange("(a p) d -> p a d", p=128),
                        in_=xb[:, :, :]))

                snap_marks()

        @block.gpsimd
        def _(gp):
            gp.wait_ge(dOUT.h, 16 * NQC)

    return nc


def kernel(L0_attn_x, L0_ffn_W_fc, L0_ffn_W_proj):
    x = np.ascontiguousarray(np.asarray(L0_attn_x, dtype=np.float32))
    wfc = np.ascontiguousarray(np.asarray(L0_ffn_W_fc, dtype=np.float32))
    wproj = np.ascontiguousarray(np.asarray(L0_ffn_W_proj, dtype=np.float32))
    B = x.shape[0]
    nc = build_program(L=x.shape[1], D=x.shape[2], HID=wfc.shape[0])
    in_maps = [{"x": x[b], "wfc": wfc, "wproj": wproj} for b in range(B)]
    res = run_bass_kernel_spmd(nc, in_maps, core_ids=list(range(B)))
    return np.stack([res.results[b]["out"] for b in range(B)], axis=0)


# version 09
# speedup vs baseline: 1.0151x; 1.0001x over previous
import numpy as np
from contextlib import ExitStack

import concourse.bass as bass
import concourse.mybir as mybir
from concourse.bass_utils import run_bass_kernel_spmd

F32 = mybir.dt.float32
BF16 = mybir.dt.bfloat16
AF = mybir.ActivationFunctionType
ALU = mybir.AluOpType
AX = mybir.AxisListType

EXP_SCALE = float(np.float32(1.0) / np.sqrt(np.float32(128.0)))
EXP_SHIFT = -10.0
EPS_LN = 1e-5
EPS_RMS = float(np.finfo(np.float32).eps)
G = 64


class Ctr:
    def __init__(self, sem, step=1):
        self.h = sem
        self.n = 0
        self.step = step

    def inc(self, ins):
        ins.then_inc(self.h, self.step)
        self.n += self.step
        return self.n


def build_program(L=4096, D=2048, HID=8192):
    H = D // 128
    NT = L // 128
    NQC = L // 512
    NDT = D // 128
    NHC = HID // 128
    W2F = min(2048, HID)
    NW2C = HID // W2F
    NT1 = HID // 128
    NTT = NT1 + NDT * NW2C
    assert L % 512 == 0 and D % 128 == 0 and HID % 128 == 0 and NT % 4 == 0

    nc = bass.Bass("TRN2", target_bir_lowering=False)
    x_hbm = nc.dram_tensor("x", [L, D], F32, kind="ExternalInput").ap()
    wfc = nc.dram_tensor("wfc", [HID, D], F32, kind="ExternalInput").ap()
    wproj = nc.dram_tensor("wproj", [D, HID], F32, kind="ExternalInput").ap()
    out_hbm = nc.dram_tensor("out", [L, D], F32, kind="ExternalOutput").ap()
    w1t = nc.dram_tensor("w1t", [NHC, 128, D], BF16, kind="Internal").ap()
    w2t = nc.dram_tensor("w2t", [NDT, 128, HID], BF16, kind="Internal").ap()
    ys = nc.dram_tensor("ys", [H, L, 129], F32, kind="Internal").ap()

    ctx = ExitStack()
    with ctx:
        ec = ctx.enter_context

        id_bf = ec(nc.sbuf_tensor("id_bf", [128, 128], BF16))
        id_f = ec(nc.sbuf_tensor("id_f", [128, 128], F32))
        tri = ec(nc.sbuf_tensor("tri", [128, 128], BF16))
        nbias = ec(nc.sbuf_tensor("nbias", [128, 1], F32))
        eps_ln = ec(nc.sbuf_tensor("eps_ln", [128, 1], F32))
        eps_rms = ec(nc.sbuf_tensor("eps_rms", [128, 1], F32))
        ones_bf = ec(nc.sbuf_tensor("ones_bf", [128, 1], BF16))
        ones_f = ec(nc.sbuf_tensor("ones_f", [1, 128], F32))

        sp = Ctr(ec(nc.semaphore("sp")))
        sa = Ctr(ec(nc.semaphore("sa")))
        sv = Ctr(ec(nc.semaphore("sv")))
        sg = Ctr(ec(nc.semaphore("sg")))
        dW = [Ctr(ec(nc.semaphore(f"dW{i}")), 16) for i in range(2)]
        dSCR = [Ctr(ec(nc.semaphore(f"dSCR{i}")), 16) for i in range(2)]
        dX = [Ctr(ec(nc.semaphore(f"dX{i}")), 16) for i in range(2)]
        dYS = [Ctr(ec(nc.semaphore(f"dYS{i}")), 16) for i in range(2)]
        dXY = Ctr(ec(nc.semaphore("dXY")), 16)
        dW1 = [Ctr(ec(nc.semaphore(f"dW1_{i}")), 16) for i in range(4)]
        dW2 = [Ctr(ec(nc.semaphore(f"dW2_{i}")), 16) for i in range(2)]
        dOUT = Ctr(ec(nc.semaphore("dOUT")), 16)

        block = ec(nc.Block())

        chain = {"tensor": sp, "vector": sv, "scalar": sa, "gpsimd": sg}
        marks = {e: 0 for e in chain}

        def snap_marks():
            for e in chain:
                marks[e] = chain[e].n

        def barrier_waits(eng, me):
            for e, c in chain.items():
                if e != me and marks[e] > 0:
                    eng.wait_ge(c.h, marks[e])

        @block.gpsimd
        def _(gp):
            def st(ins):
                sg.inc(ins)
                gp.wait_ge(sg.h, sg.n)
            st(gp.memset(id_bf[:], 0.0))
            st(gp.affine_select(out=id_bf[:], in_=id_bf[:], compare_op=ALU.not_equal,
                                fill=1.0, base=0, pattern=[[-1, 128]], channel_multiplier=1))
            st(gp.memset(id_f[:], 0.0))
            st(gp.affine_select(out=id_f[:], in_=id_f[:], compare_op=ALU.not_equal,
                                fill=1.0, base=0, pattern=[[-1, 128]], channel_multiplier=1))
            st(gp.memset(tri[:], 0.0))
            st(gp.affine_select(out=tri[:], in_=tri[:], compare_op=ALU.is_gt,
                                fill=1.0, base=0, pattern=[[-1, 128]], channel_multiplier=1))
            sg.inc(gp.memset(nbias[:], EXP_SHIFT))
            sg.inc(gp.memset(eps_ln[:], EPS_LN))
            sg.inc(gp.memset(eps_rms[:], EPS_RMS))
            sg.inc(gp.memset(ones_bf[:], 1.0))
            sg.inc(gp.memset(ones_f[:], 1.0))

        INIT_N0 = sg.n

        t_tiles = []
        for i in range(NT1):
            t_tiles.append((wfc[i * 128:(i + 1) * 128, :], w1t[i, :, :], D))
        for dt in range(NDT):
            for c in range(NW2C):
                t_tiles.append((wproj[dt * 128:(dt + 1) * 128, c * W2F:(c + 1) * W2F],
                                w2t[dt, :, c * W2F:(c + 1) * W2F], W2F))
        assert len(t_tiles) == NTT

        s1 = ExitStack()
        with s1:
            e1 = s1.enter_context
            wtf = [e1(nc.sbuf_tensor(f"wtf{i}", [128, 2048], F32)) for i in range(2)]
            wbb = [e1(nc.sbuf_tensor(f"wbb{i}", [128, 2048], BF16)) for i in range(2)]
            tgs = [e1(nc.sbuf_tensor(f"tgs{i}", [128, 2048 // G], F32)) for i in range(2)]
            tsc = [e1(nc.sbuf_tensor(f"tsc{i}", [128, 2048 // G], BF16)) for i in range(2)]
            trc = [e1(nc.sbuf_tensor(f"trc{i}", [128, 2048 // G], F32)) for i in range(2)]
            trr = [e1(nc.sbuf_tensor(f"trr{i}", [128, 2048], BF16)) for i in range(2)]
            tg1 = [e1(nc.sbuf_tensor(f"tg1{i}", [128, 2048], BF16)) for i in range(2)]
            tg2 = [e1(nc.sbuf_tensor(f"tg2{i}", [128, 2048], BF16)) for i in range(2)]
            tqt = [e1(nc.sbuf_tensor(f"tqt{i}", [128, 2048], BF16)) for i in range(4)]
            tst = [e1(nc.sbuf_tensor(f"tst{i}", [128, 16, 128], BF16,
                                       is_transpose_dest=True)) for i in range(2)]
            xres = [e1(nc.sbuf_tensor(f"xres{i}", [128, NT, 129], BF16)) for i in range(2)]
            xT = [e1(nc.sbuf_tensor(f"xT{i}", [128, L], BF16)) for i in range(2)]
            expS = [e1(nc.sbuf_tensor(f"expS{i}", [128, 1024], BF16)) for i in range(2)]
            ystg = [e1(nc.sbuf_tensor(f"ystg{i}", [128, 4, 129], F32)) for i in range(2)]
            ps_tr = [e1(nc.psum_tensor(f"ps_tr{i}", [128, 128], BF16)) for i in range(2)]
            ps_s = [e1(nc.psum_tensor(f"ps_s{i}", [128, 1024], F32)) for i in range(2)]
            ps_y = [e1(nc.psum_tensor(f"ps_y{i}", [128, 258], F32)) for i in range(2)]

            @block.gpsimd
            def _(gp):
                sg.inc(gp.memset(xres[0][:, :, 128:129], 1.0))
                sg.inc(gp.memset(xres[1][:, :, 128:129], 1.0))

            INIT_N = sg.n

            assert NTT % 2 == 0
            n_pairs = NTT // 2
            head_pairs = [[] for _ in range(H)]
            for p_ in range(n_pairs):
                head_pairs[(p_ * H) // n_pairs].append(p_)

            t_copy_pos = {}
            t_done_pos = {}
            t_ev_end = {}
            t_pe_end = {}
            tr_evict_pos = []
            tr_pe_pos = []

            def t_dve_ops(i, F):
                b = i % 2
                b4 = i % 4
                ng = F // G
                return [
                    lambda v: v.tensor_copy(wbb[b][:, 0:F], wtf[b][:, 0:F]),
                    lambda v: v.tensor_reduce(
                        tgs[b].rearrange("p (g o) -> p g o", o=1)[:, 0:ng, :],
                        wbb[b].rearrange("p (g e) -> p g e", e=G)[:, 0:ng, :],
                        AX.X, ALU.add, apply_absolute_value=True),
                    lambda v: v.tensor_scalar(tsc[b][:, 0:ng], tgs[b][:, 0:ng],
                                              1.0 / G, 1e-8, ALU.mult, ALU.max),
                    lambda v: v.reciprocal(trc[b][:, 0:ng], tsc[b][:, 0:ng]),
                    lambda v: v.tensor_tensor(
                        trr[b].rearrange("p (g e) -> p g e", e=G)[:, 0:ng, :],
                        wbb[b].rearrange("p (g e) -> p g e", e=G)[:, 0:ng, :],
                        trc[b].rearrange("p (g o) -> p g o", o=1)[:, 0:ng, :]
                           .broadcast_to([128, ng, G]),
                        ALU.mult),
                    lambda v: v.tensor_scalar(tg1[b][:, 0:F], trr[b][:, 0:F],
                                              0.5, None, ALU.is_gt),
                    lambda v: v.tensor_scalar(tg2[b][:, 0:F], trr[b][:, 0:F],
                                              -0.5, None, ALU.is_lt),
                    lambda v: v.tensor_sub(tqt[b4][:, 0:F], tg1[b][:, 0:F],
                                           tg2[b][:, 0:F]),
                    lambda v: v.tensor_tensor(
                        tqt[b4].rearrange("p (g e) -> p g e", e=G)[:, 0:ng, :],
                        tqt[b4].rearrange("p (g e) -> p g e", e=G)[:, 0:ng, :],
                        tsc[b].rearrange("p (g o) -> p g o", o=1)[:, 0:ng, :]
                           .broadcast_to([128, ng, G]),
                        ALU.mult),
                ]

            def emit_t_chunk(v, p_):
                ia, ib = 2 * p_, 2 * p_ + 1
                opsa = t_dve_ops(ia, t_tiles[ia][2])
                opsb = t_dve_ops(ib, t_tiles[ib][2])
                v.wait_ge(dW[0].h, 16 * (p_ + 1))
                v.wait_ge(dW[1].h, 16 * (p_ + 1))
                for k in range(9):
                    for i, ops in ((ia, opsa), (ib, opsb)):
                        if k > 0:
                            v.wait_ge(sv.h, sv.n - 1)
                        if k == 7 and i >= 4:
                            v.wait_ge(sp.h, t_pe_end[i - 4])
                        sv.inc(ops[k](v))
                        if k == 0:
                            assert sv.n == t_copy_pos[i]
                        if k == 8:
                            assert sv.n == t_done_pos[i]

            def emit_t_sync_loads(sy, p_):
                for i in (2 * p_, 2 * p_ + 1):
                    if i >= 2:
                        sy.wait_ge(sv.h, t_copy_pos[i - 2])
                    dW[i % 2].inc(sy.dma_start(out=wtf[i % 2][:, 0:t_tiles[i][2]],
                                               in_=t_tiles[i][0]))

            def emit_t_scratch(q, p_):
                for i in (2 * p_, 2 * p_ + 1):
                    F = t_tiles[i][2]
                    q.wait_ge(sv.h, t_ev_end[i])
                    dSCR[i % 2].inc(q.dma_start(out=t_tiles[i][1],
                                                in_=tst[i % 2][:, 0:F // 128, :]))

            def head_x_src(h):
                full = x_hbm[:, h * 128:(h + 1) * 128].rearrange("(a p) d -> p a d", p=128)
                return [full[:, q * (NT // 4):(q + 1) * (NT // 4), :] for q in range(4)]

            @block.gpsimd
            def _(gp):
                barrier_waits(gp, "gpsimd")
                for h in range(min(2, H)):
                    for q, src in enumerate(head_x_src(h)):
                        dX[h % 2].inc(gp.dma_start(
                            out=xres[h % 2][:, q * (NT // 4):(q + 1) * (NT // 4), 0:128],
                            in_=src))

            exp_abs = {}
            mask_abs = {}
            pe_head_end = []
            gks = 0
            n_masks_emitted = 0

            xt_ready = {}
            prev_yev_end = 0
            exp_abs_g = {}
            exp_par = {}
            grp_mask_gates = {}
            ggs = 0

            for h in range(H):
                pe_mark_h = sp.n
                act_mark_h = sa.n
                dve_mark_h = sv.n
                gtr0_h = len(tr_pe_pos)

                hp = head_pairs[h]
                qs = {}
                for idx, p_ in enumerate(hp):
                    if len(hp) <= max(NQC - 2, 1):
                        qc_ = min(2, NQC - len(hp)) + idx
                    else:
                        qc_ = idx // 2
                    qs[p_] = min(qc_, NQC - 1)
                slot_chunks = {qc: [] for qc in range(NQC)}
                for p_ in hp:
                    slot_chunks[qs[p_]].append(p_)
                pairs_by_region = {qc: [] for qc in range(NQC + 1)}
                for p_ in hp:
                    pairs_by_region[min(qs[p_] + 1, NQC)].append(p_)
                region_jobs = {qc: [] for qc in range(NQC + 1)}
                for p_ in hp:
                    reg = min(qs[p_] + 1, NQC)
                    for i in (2 * p_, 2 * p_ + 1):
                        for j in range(t_tiles[i][2] // 128):
                            region_jobs[reg].append(("t", i, j))
                jobs_x = ([("x", h + 1, j) for j in range(NT)] if h + 1 < H else [])
                wts = [5 * (4 * qc + 4) - 6 for qc in range(NQC)]
                tot = sum(wts)
                acc = 0
                taken = 0
                for qc in range(NQC):
                    acc += wts[qc]
                    upto = (len(jobs_x) * acc) // tot
                    region_jobs[qc] = jobs_x[taken:upto] + region_jobs[qc]
                    taken = upto
                prologue_jobs = [("x", 0, j) for j in range(NT)] if h == 0 else []

                qc_groups = []
                for qc in range(NQC):
                    gl = []
                    if qc >= 1:
                        gl += [[2 * i, 2 * i + 1] for i in range(2 * qc)]
                    gl += [[4 * qc + t0] for t0 in range(4)]
                    qc_groups.append(gl)

                pos = 0
                for j_ in prologue_jobs:
                    pos += 1
                    tr_pe_pos.append(pe_mark_h + pos)
                qc_plan = []
                for qc in range(NQC):
                    nk = 4 * qc + 4
                    groups = qc_groups[qc]
                    ng = len(groups)
                    pv_lists = [list(range(max(0, kt - 4 * qc), 4)) for kt in range(nk)]
                    jobs = region_jobs[qc]
                    share = [0] * ng
                    for x in range(len(jobs)):
                        share[(x * ng) // len(jobs)] += 1
                    seq = [("s", 0), ("s", 1)]
                    ji = 0
                    for gi in range(ng):
                        seq.append(("pv", gi))
                        for _ in range(share[gi]):
                            seq.append(("tr", jobs[ji])); ji += 1
                        if gi + 2 < ng:
                            seq.append(("s", gi + 2))
                    assert ji == len(jobs)
                    s_end = [0] * ng
                    pv_end_g = [0] * ng
                    for op in seq:
                        if op[0] == "s":
                            pos += len(groups[op[1]])
                            s_end[op[1]] = pos
                        elif op[0] == "pv":
                            for kt in groups[op[1]]:
                                pos += len(pv_lists[kt])
                            pv_end_g[op[1]] = pos
                        else:
                            pos += 1
                            tr_pe_pos.append(pe_mark_h + pos)
                            kind, a1, a2 = op[1]
                            if kind == "t" and a2 == t_tiles[a1][2] // 128 - 1:
                                t_pe_end[a1] = pe_mark_h + pos
                    qc_plan.append((groups, seq, s_end, pv_end_g, pv_lists))
                    pos = pos
                for op1 in region_jobs[NQC]:
                    pos += 1
                    tr_pe_pos.append(pe_mark_h + pos)
                    kind, a1, a2 = op1
                    if kind == "t" and a2 == t_tiles[a1][2] // 128 - 1:
                        t_pe_end[a1] = pe_mark_h + pos

                ks = gks
                nm = n_masks_emitted
                for qc in range(NQC):
                    nk = 4 * qc + 4
                    for t0 in range(4):
                        nm += 1
                        mask_abs[ks + 4 * qc + t0] = INIT_N + nm
                    ks += nk
                n_masks_total_h = nm

                a = 0
                ks = gks
                gg = ggs
                for qc in range(NQC):
                    for gi, group in enumerate(qc_groups[qc]):
                        a += 1
                        gates = []
                        for kt in group:
                            exp_abs[ks + kt] = act_mark_h + a
                            exp_par[ks + kt] = gg
                            if kt - 4 * qc >= 0:
                                gates.append(mask_abs[ks + kt])
                        exp_abs_g[gg] = act_mark_h + a
                        grp_mask_gates[gg] = gates
                        gg += 1
                    ks += 4 * qc + 4
                n_groups_h = gg - ggs

                dpos = dve_mark_h
                for j_ in prologue_jobs:
                    dpos += 1
                    tr_evict_pos.append(dpos)
                if h == 0 and prologue_jobs:
                    xt_ready[0] = dpos
                evict_pos = {}
                for qc in range(NQC):
                    for op1 in region_jobs[qc]:
                        dpos += 1
                        tr_evict_pos.append(dpos)
                        kind, a1, a2 = op1
                        if kind == "x" and a2 == NT - 1:
                            xt_ready[a1] = dpos
                        if kind == "t" and a2 == t_tiles[a1][2] // 128 - 1:
                            t_ev_end[a1] = dpos
                    for j in range(2):
                        dpos += 1
                        evict_pos[(qc, j)] = dpos
                    for p_ in slot_chunks[qc]:
                        for k in range(9):
                            for i in (2 * p_, 2 * p_ + 1):
                                dpos += 1
                                if k == 0:
                                    t_copy_pos[i] = dpos
                                if k == 8:
                                    t_done_pos[i] = dpos
                for op1 in region_jobs[NQC]:
                    dpos += 1
                    tr_evict_pos.append(dpos)
                    kind, a1, a2 = op1
                    if kind == "x" and a2 == NT - 1:
                        xt_ready[a1] = dpos
                    if kind == "t" and a2 == t_tiles[a1][2] // 128 - 1:
                        t_ev_end[a1] = dpos

                @block.tensor
                def _(t, h=h, qc_plan=qc_plan, pe_mark_h=pe_mark_h, gks=gks,
                      ggs=ggs, region_jobs=region_jobs,
                      prologue_jobs=prologue_jobs, gtr0_h=gtr0_h,
                      prev_yev_end=prev_yev_end, evict_pos=evict_pos):
                    if h == 0:
                        barrier_waits(t, "tensor")
                        t.wait_ge(dX[0].h, 16 * 4)
                    g = gtr0_h
                    seen_x = set()
                    seen_t = set()

                    def emit_tr(op1, g):
                        kind, a1, a2 = op1
                        if kind == "x":
                            if a1 not in seen_x:
                                t.wait_ge(dX[a1 % 2].h, 16 * 4 * (a1 // 2 + 1))
                                seen_x.add(a1)
                            src = xres[a1 % 2][:, a2, 0:128]
                        else:
                            if a1 not in seen_t:
                                t.wait_ge(sv.h, t_done_pos[a1])
                                seen_t.add(a1)
                            src = tqt[a1 % 4][:, a2 * 128:(a2 + 1) * 128]
                        if g >= 2:
                            t.wait_ge(sv.h, tr_evict_pos[g - 2])
                        sp.inc(t.transpose(ps_tr[g % 2][:], src, id_bf[:]))
                        assert sp.n == tr_pe_pos[g]

                    for op1 in prologue_jobs:
                        emit_tr(op1, g); g += 1
                    ks = gks
                    gg = ggs
                    for qc in range(NQC):
                        groups, seq, s_end, pv_end_g, pv_lists = qc_plan[qc]

                        def emit_s(gi, ks=ks, qc=qc, gg=gg, groups=groups):
                            gx = gg + gi
                            if gx >= 2:
                                t.wait_ge(sa.h, exp_abs_g[gx - 2])
                            if gi == 0 and qc == 0:
                                t.wait_ge(sv.h, xt_ready[h])
                            for idx, kt in enumerate(groups[gi]):
                                qlo = 128 * max(0, kt - 4 * qc)
                                sp.inc(t.matmul(
                                    ps_s[gx % 2][:, 512 * idx + qlo: 512 * idx + 512],
                                    xT[h % 2][:, kt * 128:(kt + 1) * 128],
                                    xT[h % 2][:, qc * 512 + qlo: qc * 512 + 512],
                                    start=True, stop=True))

                        def emit_pv(gi, ks=ks, qc=qc, gg=gg, groups=groups,
                                    pv_lists=pv_lists):
                            gx = gg + gi
                            first = True
                            for idx, kt in enumerate(groups[gi]):
                                kk = ks + kt
                                t0 = kt - 4 * qc
                                firstkt = True
                                for u in pv_lists[kt]:
                                    if first:
                                        t.wait_ge(sa.h, exp_abs[kk])
                                        if kt == 0 and qc >= 1:
                                            t.wait_ge(sv.h, evict_pos[(qc - 1, 1)])
                                        if kt == 0 and qc == 0 and prev_yev_end > 0:
                                            t.wait_ge(sv.h, prev_yev_end)
                                        first = False
                                    if firstkt and t0 >= 0:
                                        t.wait_ge(sg.h, mask_abs[kk])
                                    firstkt = False
                                    sp.inc(t.matmul(
                                        ps_y[u // 2][:, (u % 2) * 129:(u % 2) * 129 + 129],
                                        expS[gx % 2][:, 512 * idx + u * 128: 512 * idx + (u + 1) * 128],
                                        xres[h % 2][:, kt, 0:129],
                                        start=(kt == 0 and u % 2 == 0),
                                        stop=(u % 2 == 1 and kt == 4 * qc + u)))

                        for op in seq:
                            if op[0] == "s":
                                emit_s(op[1])
                            elif op[0] == "pv":
                                emit_pv(op[1])
                            else:
                                emit_tr(op[1], g); g += 1
                        ks += 4 * qc + 4
                        gg += len(groups)
                    for op1 in region_jobs[NQC]:
                        emit_tr(op1, g); g += 1

                @block.scalar
                def _(s, h=h, qc_plan=qc_plan, pe_mark_h=pe_mark_h, gks=gks,
                      ggs=ggs):
                    if h == 0:
                        barrier_waits(s, "scalar")
                    ks = gks
                    gg = ggs
                    for qc in range(NQC):
                        groups, seq, s_end, pv_end_g, pv_lists = qc_plan[qc]
                        for gi, group in enumerate(groups):
                            gx = gg + gi
                            s.wait_ge(sp.h, pe_mark_h + s_end[gi])
                            for gate in grp_mask_gates.get(gx - 2, ()):
                                s.wait_ge(sg.h, gate)
                            if len(group) == 2:
                                sa.inc(s.activation(expS[gx % 2][:, 0:1024],
                                                    ps_s[gx % 2][:, 0:1024],
                                                    AF.Exp, bias=nbias[:],
                                                    scale=EXP_SCALE))
                            else:
                                kt = group[0]
                                qlo = 128 * max(0, kt - 4 * qc)
                                sa.inc(s.activation(expS[gx % 2][:, qlo:512],
                                                    ps_s[gx % 2][:, qlo:512],
                                                    AF.Exp, bias=nbias[:],
                                                    scale=EXP_SCALE))
                        ks += 4 * qc + 4
                        gg += len(groups)

                @block.vector
                def _(v, h=h, qc_plan=qc_plan, pe_mark_h=pe_mark_h,
                      region_jobs=region_jobs, prologue_jobs=prologue_jobs,
                      gtr0_h=gtr0_h, evict_pos=evict_pos,
                      slot_chunks=slot_chunks):
                    if h == 0:
                        barrier_waits(v, "vector")
                        v.wait_ge(sg.h, INIT_N)
                    g = gtr0_h
                    seen_t = set()

                    def emit_cp(op1, g):
                        kind, a1, a2 = op1
                        v.wait_ge(sp.h, tr_pe_pos[g])
                        if kind == "x":
                            sv.inc(v.tensor_copy(xT[a1 % 2][:, a2 * 128:(a2 + 1) * 128],
                                                 ps_tr[g % 2][:]))
                        else:
                            if a1 not in seen_t and a1 >= 2:
                                v.wait_ge(dSCR[a1 % 2].h, 16 * ((a1 - 2) // 2 + 1))
                            seen_t.add(a1)
                            sv.inc(v.tensor_copy(tst[a1 % 2][:, a2, :],
                                                 ps_tr[g % 2][:]))
                        assert sv.n == tr_evict_pos[g]

                    for op1 in prologue_jobs:
                        emit_cp(op1, g); g += 1
                    for qc in range(NQC):
                        groups, seq, s_end, pv_end_g, pv_lists = qc_plan[qc]
                        for op1 in region_jobs[qc]:
                            emit_cp(op1, g); g += 1
                        for j in range(2):
                            gi_last = groups.index([4 * qc + 2 * j + 1])
                            v.wait_ge(sp.h, pe_mark_h + pv_end_g[gi_last])
                            if j == 0:
                                idx_par = (NQC // 2) * h + qc // 2
                                if idx_par >= 1:
                                    v.wait_ge(dYS[qc % 2].h, 16 * idx_par)
                            sv.inc(v.tensor_copy(
                                ystg[qc % 2][:, 2 * j:2 * j + 2, :],
                                ps_y[j].rearrange("p (a e) -> p a e", e=129)))
                            assert sv.n == evict_pos[(qc, j)]
                        for p_ in slot_chunks[qc]:
                            emit_t_chunk(v, p_)
                    for op1 in region_jobs[NQC]:
                        emit_cp(op1, g); g += 1

                @block.gpsimd
                def _(gp, h=h, gks=gks, evict_pos=evict_pos):
                    ks = gks
                    for qc in range(NQC):
                        nk = 4 * qc + 4
                        for t0 in range(4):
                            kk = ks + 4 * qc + t0
                            gp.wait_ge(sa.h, exp_abs[kk])
                            ins = gp.tensor_tensor(
                                expS[exp_par[kk] % 2][:, t0 * 128:(t0 + 1) * 128],
                                expS[exp_par[kk] % 2][:, t0 * 128:(t0 + 1) * 128],
                                tri[:], ALU.mult)
                            sg.inc(ins)
                            assert sg.n == mask_abs[kk]
                        gp.wait_ge(sv.h, evict_pos[(qc, 1)])
                        dYS[qc % 2].inc(gp.dma_start(
                            out=ys[h].rearrange("(a p) e -> p a e", p=128)
                            [:, 4 * qc:4 * qc + 4, :],
                            in_=ystg[qc % 2][:, :, :]))
                        ks += nk

                @block.sync
                def _(sy, h=h):
                    if h == 0:
                        barrier_waits(sy, "sync")
                    items = []
                    for p_ in head_pairs[h]:
                        gl = max((t_copy_pos[i - 2] for i in (2 * p_, 2 * p_ + 1)
                                  if i >= 2), default=0)
                        items.append((gl, 0, p_))
                        gs_ = max(t_ev_end[i] for i in (2 * p_, 2 * p_ + 1))
                        items.append((gs_, 1, p_))
                    for _, kind2, p_ in sorted(items):
                        if kind2 == 0:
                            emit_t_sync_loads(sy, p_)
                        else:
                            emit_t_scratch(sy, p_)

                gks += sum(4 * qc + 4 for qc in range(NQC))
                ggs += n_groups_h
                n_masks_emitted = n_masks_total_h
                pe_head_end.append(sp.n)
                prev_yev_end = evict_pos[(NQC - 1, 1)]

                if h + 2 < H:
                    @block.gpsimd
                    def _(gp, h=h):
                        gp.wait_ge(sp.h, pe_head_end[h])
                        for q, src in enumerate(head_x_src(h + 2)):
                            dX[h % 2].inc(gp.dma_start(
                                out=xres[h % 2][:, q * (NT // 4):(q + 1) * (NT // 4), 0:128],
                                in_=src))

            snap_marks()

        s2 = ExitStack()
        with s2:
            e2 = s2.enter_context
            xb = e2(nc.sbuf_tensor("xb", [128, 4, D], F32))
            yb = e2(nc.sbuf_tensor("yb", [128, 4, H, 129], F32))
            rcp16 = e2(nc.sbuf_tensor("rcp16", [128, H], F32))
            X1T = e2(nc.sbuf_tensor("X1T", [128, NDT, 512], BF16))
            hbuf = e2(nc.sbuf_tensor("hbuf", [128, NHC, 512], BF16))
            w1b = [e2(nc.sbuf_tensor(f"w1b{i}", [128, D], BF16)) for i in range(4)]
            w2b = [e2(nc.sbuf_tensor(f"w2b{i}", [128, HID // 2], BF16)) for i in range(2)]
            trel = [e2(nc.sbuf_tensor(f"trel{i}", [128, 512], BF16)) for i in range(2)]
            hsq = [e2(nc.sbuf_tensor(f"hsq{i}", [128, 512], BF16)) for i in range(2)]
            lnt = e2(nc.sbuf_tensor("lnt", [1, 512], F32))
            rstd_row = e2(nc.sbuf_tensor("rstd_row", [1, 512], F32))
            rstd_b = e2(nc.sbuf_tensor("rstd_b", [128, 512], F32))
            yTd = [e2(nc.sbuf_tensor(f"yTd{i}", [128, 512], F32)) for i in range(2)]
            sqb = e2(nc.sbuf_tensor("sqb", [128, D], F32))
            s1c = e2(nc.sbuf_tensor("s1c", [128, 1], F32))
            s2c = e2(nc.sbuf_tensor("s2c", [128, 1], F32))
            mc = e2(nc.sbuf_tensor("mc", [128, 1], F32))
            m2c = e2(nc.sbuf_tensor("m2c", [128, 1], F32))
            vc = e2(nc.sbuf_tensor("vc", [128, 1], F32))
            rsc = e2(nc.sbuf_tensor("rsc", [128, 1], F32))
            ps_h = [e2(nc.psum_tensor(f"ps_h{i}", [128, 512], F32)) for i in range(2)]
            ps_m = [e2(nc.psum_tensor(f"ps_m{i}", [128, 512], F32)) for i in range(2)]
            ps_den = e2(nc.psum_tensor("ps_den", [128, 512], F32))
            ps_bc = e2(nc.psum_tensor("ps_bc", [128, 512], F32))
            ps_tf = [e2(nc.psum_tensor(f"ps_tf{i}", [128, 128], F32)) for i in range(2)]

            for c in range(NQC):
                pe_b, sa_b, sv_b, dxy_b = sp.n, sa.n, sv.n, dXY.n

                @block.sync
                def _(sy, c=c):
                    barrier_waits(sy, "sync")
                    if c == 0:
                        sy.wait_ge(dYS[0].h, 16 * (NQC // 2) * H)
                        sy.wait_ge(dYS[1].h, 16 * (NQC // 2) * H)
                    if c >= 1:
                        sy.wait_ge(dOUT.h, 16 * c)
                    dXY.inc(sy.dma_start(
                        out=xb[:, :, :],
                        in_=x_hbm[c * 512:(c + 1) * 512, :]
                        .rearrange("(a p) d -> p a d", p=128)))
                    for tp in range(4):
                        for h in range(H):
                            dXY.inc(sy.dma_start(
                                out=yb[:, tp, h, :],
                                in_=ys[h].rearrange("(a p) e -> p a e", p=128)
                                [:, 4 * c + tp, :]))

                def bp(tp, j, sv_b=sv_b, H=H):
                    return sv_b + tp * (H + 9) + j

                @block.vector
                def _(v, c=c, dxy_b=dxy_b):
                    def st(ins):
                        sv.inc(ins)
                        v.wait_ge(sv.h, sv.n)
                    barrier_waits(v, "vector")
                    v.wait_ge(dXY.h, dxy_b + 16 * (1 + 4 * H))
                    for tp in range(4):
                        st(v.reciprocal(rcp16[:, :], yb[:, tp, :, 128]))
                        for h in range(H):
                            st(v.scalar_tensor_tensor(
                                xb[:, tp, h * 128:(h + 1) * 128],
                                yb[:, tp, h, 0:128],
                                rcp16[:, h:h + 1],
                                xb[:, tp, h * 128:(h + 1) * 128],
                                ALU.mult, ALU.add))
                        st(v.tensor_reduce(s1c[:], xb[:, tp, :], AX.X, ALU.add))
                        st(v.tensor_scalar(mc[:], s1c[:], 1.0 / D, None, ALU.mult))
                        st(v.tensor_tensor(sqb[:, :], xb[:, tp, :], xb[:, tp, :],
                                           ALU.mult))
                        st(v.tensor_reduce(s2c[:], sqb[:, :], AX.X, ALU.add))
                        st(v.tensor_scalar(s2c[:], s2c[:], 1.0 / D, None, ALU.mult))
                        st(v.tensor_tensor(m2c[:], mc[:], mc[:], ALU.mult))
                        st(v.tensor_sub(vc[:], s2c[:], m2c[:]))
                        v.wait_ge(sa.h, sa_b + 2 * tp + 2)
                        st(v.tensor_scalar(xb[:, tp, :], xb[:, tp, :], mc[:], rsc[:],
                                           ALU.subtract, ALU.mult))

                @block.scalar
                def _(s, c=c):
                    def st(ins):
                        sa.inc(ins)
                        s.wait_ge(sa.h, sa.n)
                    barrier_waits(s, "scalar")
                    for tp in range(4):
                        s.wait_ge(sv.h, bp(tp, H + 8))
                        st(s.activation(vc[:], vc[:], AF.Ln, bias=eps_ln[:]))
                        st(s.activation(rsc[:], vc[:], AF.Exp, scale=-0.5))

                pe_bT = sp.n

                @block.tensor
                def _(t, c=c):
                    barrier_waits(t, "tensor")
                    g = 0
                    for tq in range(4):
                        for dt in range(NDT):
                            if dt == 0:
                                t.wait_ge(sv.h, bp(tq, H + 9))
                            if g >= 2:
                                t.wait_ge(sa.h, sa_b + 8 + g - 1)
                            sp.inc(t.transpose(ps_tf[g % 2][:],
                                               xb[:, tq, dt * 128:(dt + 1) * 128],
                                               id_f[:]))
                            g += 1

                @block.scalar
                def _(s, c=c, pe_bT=pe_bT):
                    g = 0
                    for tq in range(4):
                        for dt in range(NDT):
                            s.wait_ge(sp.h, pe_bT + g + 1)
                            sa.inc(s.copy(X1T[:, dt, tq * 128:(tq + 1) * 128],
                                          ps_tf[g % 2][:]))
                            g += 1

                snap_marks()

                pe_c, sa_c, sv_c = sp.n, sa.n, sv.n
                dw1_c = [c_.n for c_ in dW1]
                dw2_c = [dW2[0].n, dW2[1].n]

                pe_ops = []
                for hc in range(NHC):
                    pe_ops.append(("m1", hc))
                    if hc >= 1:
                        pe_ops.append(("den", hc - 1))
                pe_ops.append(("den", NHC - 1))
                pe_ops.append(("bcast", 0))
                tr_order = []
                for dt in range(NDT):
                    pe_ops.append(("m2", dt))
                    if dt >= 2:
                        pe_ops.append(("tr", dt - 2))
                        tr_order.append(dt - 2)
                pe_ops.append(("tr", NDT - 2)); tr_order.append(NDT - 2)
                pe_ops.append(("tr", NDT - 1)); tr_order.append(NDT - 1)

                m1_end = {}
                den_pos = {}
                m2_end = {}
                trp = {}
                bcast_pos = 0
                p = 0
                for kind, idx in pe_ops:
                    if kind == "m1":
                        p += NDT; m1_end[idx] = p
                    elif kind == "den":
                        p += 1; den_pos[idx] = p
                    elif kind == "bcast":
                        p += 1; bcast_pos = p
                    elif kind == "m2":
                        p += NHC; m2_end[idx] = p
                    elif kind == "tr":
                        for tq in range(4):
                            p += 1; trp[(idx, tq)] = p

                def d_tt(hc, sv_c=sv_c):
                    return sv_c + 2 * hc + 1

                def d_hsq(hc, sv_c=sv_c):
                    return sv_c + 2 * hc + 2

                d_bc = sv_c + 2 * NHC + 1

                def d_ev(dt, d_bc=d_bc):
                    return d_bc + 5 * dt + 1

                def d_add(dt, tq, d_bc=d_bc):
                    return d_bc + 5 * dt + 2 + tq

                d_addend = d_bc + 5 * NDT

                def d_ln2(tq, j, d_addend=d_addend):
                    return d_addend + 8 * tq + j

                def a_relu(hc, sa_c=sa_c):
                    return sa_c + hc + 1

                a_rms_done = sa_c + NHC + 2

                def a_ln2(tq, a_rms_done=a_rms_done):
                    return a_rms_done + 2 * tq + 1

                @block.sync
                def _(sy, c=c):
                    barrier_waits(sy, "sync")
                    if c == 0:
                        sy.wait_ge(dSCR[0].h, 16 * ((NTT + 1) // 2))
                        sy.wait_ge(dSCR[1].h, 16 * (NTT // 2))
                    for hc in range(NHC):
                        if hc >= 4:
                            sy.wait_ge(sp.h, pe_c + m1_end[hc - 4])
                        dW1[hc % 4].inc(sy.dma_start(out=w1b[hc % 4][:, :], in_=w1t[hc, :, :]))
                    for dt in range(NDT):
                        for half in range(2):
                            if dt >= 1:
                                prev = m2_end[dt - 1] - (NHC // 2 if half == 0 else 0)
                                sy.wait_ge(sp.h, pe_c + prev)
                            dW2[half].inc(sy.dma_start(
                                out=w2b[half][:, :],
                                in_=w2t[dt, :, half * (HID // 2):(half + 1) * (HID // 2)]))

                @block.tensor
                def _(t, c=c):
                    barrier_waits(t, "tensor")
                    for kind, idx in pe_ops:
                        if kind == "m1":
                            hc = idx
                            t.wait_ge(dW1[hc % 4].h, dw1_c[hc % 4] + 16 * (hc // 4 + 1))
                            if hc >= 2:
                                t.wait_ge(sa.h, a_relu(hc - 2))
                            for dt in range(NDT):
                                sp.inc(t.matmul(ps_h[hc % 2][:, :],
                                                w1b[hc % 4][:, dt * 128:(dt + 1) * 128],
                                                X1T[:, dt, :],
                                                start=(dt == 0), stop=(dt == NDT - 1)))
                        elif kind == "den":
                            hc = idx
                            t.wait_ge(sv.h, d_hsq(hc))
                            sp.inc(t.matmul(ps_den[0:1, :], ones_bf[:],
                                            hsq[hc % 2][:, :],
                                            start=(hc == 0), stop=(hc == NHC - 1)))
                        elif kind == "bcast":
                            t.wait_ge(sa.h, a_rms_done)
                            sp.inc(t.matmul(ps_bc[:, :], ones_f[:, :], rstd_row[:, :],
                                            start=True, stop=True))
                        elif kind == "m2":
                            dt = idx
                            for hc in range(NHC):
                                half = 0 if hc < NHC // 2 else 1
                                if hc == 0 or hc == NHC // 2:
                                    t.wait_ge(dW2[half].h, dw2_c[half] + 16 * (dt + 1))
                                if hc == 0 and dt >= 2:
                                    t.wait_ge(sv.h, d_ev(dt - 2))
                                off = hc - half * (NHC // 2)
                                sp.inc(t.matmul(ps_m[dt % 2][:, :],
                                                w2b[half][:, off * 128:(off + 1) * 128],
                                                hbuf[:, hc, :],
                                                start=(hc == 0), stop=(hc == NHC - 1)))
                        elif kind == "tr":
                            dt = idx
                            oi = tr_order.index(dt)
                            for tq in range(4):
                                t.wait_ge(sv.h, d_ev(dt))
                                gt = 4 * oi + tq
                                if gt >= 2:
                                    pdt, ptq = tr_order[(gt - 2) // 4], (gt - 2) % 4
                                    t.wait_ge(sv.h, d_add(pdt, ptq))
                                sp.inc(t.transpose(ps_tf[tq % 2][:],
                                                   yTd[dt % 2][:, tq * 128:(tq + 1) * 128],
                                                   id_f[:]))

                @block.scalar
                def _(s, c=c):
                    def st(ins):
                        sa.inc(ins)
                        s.wait_ge(sa.h, sa.n)
                    barrier_waits(s, "scalar")
                    for hc in range(NHC):
                        s.wait_ge(sp.h, pe_c + m1_end[hc])
                        if hc >= 2:
                            s.wait_ge(sv.h, d_tt(hc - 2))
                        sa.inc(s.activation(trel[hc % 2][:, :], ps_h[hc % 2][:, :],
                                            AF.Relu))
                    s.wait_ge(sp.h, pe_c + den_pos[NHC - 1])
                    st(s.activation(lnt[:, :], ps_den[0:1, :], AF.Ln,
                                    bias=eps_rms[0:1, :], scale=1.0 / HID))
                    st(s.activation(rstd_row[:, :], lnt[:, :], AF.Exp, scale=-0.5))
                    for tq in range(4):
                        s.wait_ge(sv.h, d_ln2(tq, 7))
                        st(s.activation(vc[:], vc[:], AF.Ln, bias=eps_ln[:]))
                        st(s.activation(rsc[:], vc[:], AF.Exp, scale=-0.5))

                @block.vector
                def _(v, c=c):
                    def st(ins):
                        sv.inc(ins)
                        v.wait_ge(sv.h, sv.n)
                    barrier_waits(v, "vector")
                    for hc in range(NHC):
                        v.wait_ge(sa.h, a_relu(hc))
                        if hc >= 2:
                            v.wait_ge(sp.h, pe_c + den_pos[hc - 2])
                        st(v.tensor_tensor(hbuf[:, hc, :], trel[hc % 2][:, :],
                                           trel[hc % 2][:, :], ALU.mult))
                        st(v.tensor_tensor(hsq[hc % 2][:, :], hbuf[:, hc, :],
                                           hbuf[:, hc, :], ALU.mult))
                    v.wait_ge(sp.h, pe_c + bcast_pos)
                    st(v.tensor_copy(rstd_b[:, :], ps_bc[:, :]))
                    for dt in range(NDT):
                        v.wait_ge(sp.h, pe_c + m2_end[dt])
                        if dt >= 2:
                            v.wait_ge(sp.h, pe_c + trp[(dt - 2, 3)])
                        st(v.tensor_tensor(yTd[dt % 2][:, :], ps_m[dt % 2][:, :],
                                           rstd_b[:, :], ALU.mult))
                        for tq in range(4):
                            v.wait_ge(sp.h, pe_c + trp[(dt, tq)])
                            st(v.tensor_tensor(xb[:, tq, dt * 128:(dt + 1) * 128],
                                               ps_tf[tq % 2][:],
                                               xb[:, tq, dt * 128:(dt + 1) * 128],
                                               ALU.add))
                    for tq in range(4):
                        st(v.tensor_reduce(s1c[:], xb[:, tq, :], AX.X, ALU.add))
                        st(v.tensor_scalar(mc[:], s1c[:], 1.0 / D, None, ALU.mult))
                        st(v.tensor_tensor(sqb[:, :], xb[:, tq, :], xb[:, tq, :],
                                           ALU.mult))
                        st(v.tensor_reduce(s2c[:], sqb[:, :], AX.X, ALU.add))
                        st(v.tensor_scalar(s2c[:], s2c[:], 1.0 / D, None, ALU.mult))
                        st(v.tensor_tensor(m2c[:], mc[:], mc[:], ALU.mult))
                        st(v.tensor_sub(vc[:], s2c[:], m2c[:]))
                        v.wait_ge(sa.h, a_rms_done + 2 * (tq + 1))
                        st(v.tensor_scalar(xb[:, tq, :], xb[:, tq, :], mc[:], rsc[:],
                                           ALU.subtract, ALU.mult))

                @block.gpsimd
                def _(gp, c=c):
                    gp.wait_ge(sv.h, d_ln2(3, 8))
                    dOUT.inc(gp.dma_start(
                        out=out_hbm[c * 512:(c + 1) * 512, :]
                        .rearrange("(a p) d -> p a d", p=128),
                        in_=xb[:, :, :]))

                snap_marks()

        @block.gpsimd
        def _(gp):
            gp.wait_ge(dOUT.h, 16 * NQC)

    return nc


def kernel(L0_attn_x, L0_ffn_W_fc, L0_ffn_W_proj):
    x = np.ascontiguousarray(np.asarray(L0_attn_x, dtype=np.float32))
    wfc = np.ascontiguousarray(np.asarray(L0_ffn_W_fc, dtype=np.float32))
    wproj = np.ascontiguousarray(np.asarray(L0_ffn_W_proj, dtype=np.float32))
    B = x.shape[0]
    nc = build_program(L=x.shape[1], D=x.shape[2], HID=wfc.shape[0])
    in_maps = [{"x": x[b], "wfc": wfc, "wproj": wproj} for b in range(B)]
    res = run_bass_kernel_spmd(nc, in_maps, core_ids=list(range(B)))
    return np.stack([res.results[b]["out"] for b in range(B)], axis=0)
